# version 12
# baseline (speedup 1.0000x reference)
import contextlib
import numpy as np
import ml_dtypes
import concourse.bass as bass
import concourse.mybir as mybir
from concourse.bass_utils import run_bass_kernel_spmd

F32 = mybir.dt.float32
BF16 = mybir.dt.bfloat16
AF = mybir.ActivationFunctionType
ALU = mybir.AluOpType
AX = mybir.AxisListType

T = 2048
D = 2048
NT = 16
TC = 512
NCH = T // TC

STREAMS = ("pe", "act", "dve", "pool", "sp")
CSEM = {"pe": "pe", "act": "act", "dve": "dve", "pool": "pool"}
SEMS = ("pe", "act", "dve", "pool", "spq0", "spq1", "poolq0", "poolq1")


def MM(out, lhsT, rhs, start, stop):
    return lambda e: e.matmul(out, lhsT, rhs, start=start, stop=stop,
                              skip_group_check=True)


def CP(out, in_):
    return lambda e: e.tensor_copy(out, in_)


def TT(out, a, b, op):
    return lambda e: e.tensor_tensor(out, a, b, op)


def TS1(out, in_, s, op):
    return lambda e: e.tensor_single_scalar(out, in_, s, op)


def TS2(out, in_, s1, s2, op0, op1):
    return lambda e: e.tensor_scalar(out, in_, s1, s2, op0, op1)


def STT(out, in0, s, in1, op0, op1):
    return lambda e: e.scalar_tensor_tensor(out, in0, s, in1, op0, op1)


def ACTV(out, in_, fn, bias=0.0, scale=1.0):
    return lambda e: e.activation(out, in_, fn, bias=bias, scale=scale)


def RMAX(out, in_, negate=False):
    return lambda e: e.reduce_max(out, in_, AX.X, negate=negate)


def RSUM(out, in_, absval=False):
    return lambda e: e.reduce_sum(out, in_, AX.X, apply_absolute_value=absval)


def RECIP(out, in_):
    return lambda e: e.reciprocal(out, in_)


def RSUMC(out, in_):
    return lambda e: e.reduce_sum(out, in_, AX.C)


def DMA(out, in_):
    return lambda e: e.dma_start(out, in_)


def MEMSET(ap, v):
    return lambda e: e.memset(ap, v)


class Plan:

    def __init__(self):
        self.ops = {s: [] for s in STREAMS}
        self.cnt = dict.fromkeys(SEMS, 0)
        self.ep = {"sp": 0, "pool": 0}
        self.ep_open = {"sp": False, "pool": False}
        self.ep_end = dict.fromkeys(
            [("sp", 0), ("sp", 1), ("pool", 0), ("pool", 1)], 0)
        self.bar = None

    def _push(self, stream, waits, fn, inc_sem, inc):
        own = CSEM.get(stream)
        if own in ("act", "dve", "pool"):
            waits = list(waits) + [(own, self.cnt[own])]
        if self.bar is not None:
            waits = list(waits) + self.bar
        self.ops[stream].append((list(waits), fn, inc_sem, inc))
        if inc_sem is not None:
            self.cnt[inc_sem] += inc
            return self.cnt[inc_sem]
        return None

    def pe(self, fn, waits=(), tick=False):
        return self._push("pe", waits, fn, "pe" if tick else None, 1)

    def act(self, fn, waits=(), tick=False):
        return self._push("act", waits, fn, "act" if tick else None, 1)

    def dve(self, fn, waits=(), tick=False):
        return self._push("dve", waits, fn, "dve" if tick else None, 1)

    def pool(self, fn, waits=(), tick=False):
        return self._push("pool", waits, fn, "pool" if tick else None, 1)

    def _qsem(self, stream):
        return ("spq%d" if stream == "sp" else "poolq%d") % (self.ep[stream] % 2)

    def dma(self, stream, fn, waits=()):
        qn = self._qsem(stream)
        waits = list(waits)
        if not self.ep_open[stream]:
            waits.append((qn, self.ep_end[(stream, self.ep[stream] % 2)]))
            self.ep_open[stream] = True
        return self._push(stream, waits, fn, qn, 16)

    def end_ep(self, stream):
        qn = self._qsem(stream)
        tokv = self.cnt[qn]
        self.ep_end[(stream, self.ep[stream] % 2)] = tokv
        self.ep[stream] += 1
        self.ep_open[stream] = False
        return (qn, tokv)

    def dma1(self, stream, fn, waits=()):
        self.dma(stream, fn, waits)
        return self.end_ep(stream)

    def set_barrier(self):
        self.bar = [(s, self.cnt[s]) for s in SEMS]

    def tok(self, sem):
        return self.cnt[sem]

    GO = 3_000_000

    def emit(self, nc):
        with contextlib.ExitStack() as st:
            sems = {s: st.enter_context(nc.semaphore("s_" + s)) for s in SEMS}
            go = st.enter_context(nc.semaphore("s_go"))
            block = st.enter_context(nc.Block())

            def run_stream(eng, stream):
                if stream == "pool":
                    eng.sem_clear(go)
                    for s in SEMS:
                        eng.sem_clear(sems[s])
                    eng.sem_inc(go, Plan.GO)
                else:
                    eng.wait_ge(go, Plan.GO)
                seen = dict.fromkeys(SEMS, 0)
                for waits, fn, inc_sem, inc in self.ops[stream]:
                    for wsem, wval in waits:
                        if wval <= 0:
                            continue
                        if wval > seen[wsem]:
                            eng.wait_ge(sems[wsem], wval)
                            seen[wsem] = wval
                    instr = fn(eng)
                    if inc_sem is not None:
                        instr.then_inc(sems[inc_sem], inc)

            @block.tensor
            def _(t):
                run_stream(t, "pe")

            @block.scalar
            def _(s):
                run_stream(s, "act")

            @block.vector
            def _(v):
                run_stream(v, "dve")

            @block.gpsimd
            def _(g):
                run_stream(g, "pool")

            @block.sync
            def _(sy):
                run_stream(sy, "sp")
                for sem in SEMS:
                    if self.cnt[sem] > 0:
                        sy.wait_ge(sems[sem], self.cnt[sem])
        with contextlib.ExitStack() as st2:
            block2 = st2.enter_context(nc.Block())

            @block2.gpsimd
            def _(g):
                g.sem_clear(go)
                for sem in SEMS:
                    g.sem_clear(sems[sem])
        return nc


AR_COLS = 78336


class Bufs:
    def __init__(self, nc, stack):
        ctx = stack.enter_context
        self.arena = ctx(nc.sbuf_tensor("arena", [128, AR_COLS], BF16))
        self.arena_f = self.arena.bitcast(F32)
        self.qk = ctx(nc.sbuf_tensor("qk", [128, 6144], BF16))
        self.ev = ctx(nc.sbuf_tensor("ev", [128, 2048], F32))
        self.evb = ctx(nc.sbuf_tensor("evb", [128, 1024], BF16))
        self.srA = ctx(nc.sbuf_tensor("srA", [1, 1536], F32))
        self.srB = ctx(nc.sbuf_tensor("srB", [1, 4096], BF16))
        self.colA = ctx(nc.sbuf_tensor("colA", [128, 16], F32))
        self.colB = ctx(nc.sbuf_tensor("colB", [128, 8], F32))
        self.ones_b = ctx(nc.sbuf_tensor("ones_b", [128, 512], BF16))
        self.ones_f = ctx(nc.sbuf_tensor("ones_f", [1, 128], F32))
        self.msk = ctx(nc.sbuf_tensor("msk", [128, 2048], BF16))
        self.gsc = ctx(nc.sbuf_tensor("gsc", [8, 2048], F32))
        self.sel = ctx(nc.sbuf_tensor("sel", [8, 1024], F32))
        self.ps = [ctx(nc.psum_tensor("ps%d" % i, [128, 512], F32))
                   for i in range(8)]

    def bf(self, off, cols):
        assert off % 2 == 0 and off // 2 + cols <= AR_COLS, (off, cols)
        return self.arena[:, off // 2: off // 2 + cols]

    def f32(self, off, cols):
        assert off % 4 == 0 and off // 4 + cols <= AR_COLS // 2, (off, cols)
        return self.arena_f[:, off // 4: off // 4 + cols]


Y_OFF = 0
A2 = 65536
A3 = 131072
A4 = 155648
AR_END = 156672
assert AR_END == AR_COLS * 2


def ysl(B, kd, c0, cn):
    return B.bf(Y_OFF + kd * 2 * T + 2 * c0, cn)


def setup_const(p, B):
    p.pool(MEMSET(B.ones_b[:, :], 1.0), tick=True)
    p.k_const = p.pool(MEMSET(B.ones_f[:, :], 1.0), tick=True)
    return p.k_const


def emit_ln(p, B, xin):
    xf = [B.f32(A2 + kd * 2048, TC) for kd in range(NT)]
    t1 = [B.f32(A2 + 32768 + kd * 2048, TC) for kd in range(NT)]
    xb = [B.bf(A3 + kd * 1024, TC) for kd in range(NT)]
    sq = xb
    ps_s, ps_r = B.ps[0], B.ps[1]
    mrow = B.srA[0:1, 0:512]
    vrow = B.srA[0:1, 512:1024]
    sd = B.srA[0:1, 1024:1536]

    prev_done = p.tok("dve")
    for tc in range(NCH):
        t0 = tc * TC
        for kd in range(NT):
            src = xin.ap()[kd * 128:(kd + 1) * 128, t0:t0 + TC]
            p.dma("sp", DMA(xf[kd], src), waits=[("dve", prev_done)])
        tok_f = p.end_ep("sp")
        for kd in range(NT):
            src = xin.ap()[kd * 128:(kd + 1) * 128, t0:t0 + TC]
            p.dma("pool", DMA(xb[kd], src),
                  waits=[("dve", prev_done), ("act", p.tok("act"))])
        tok_b = p.end_ep("pool")
        for kd in range(NT):
            p.pe(MM(ps_s[0:1, :], B.ones_b[:, 0:1], xb[kd], kd == 0, kd == NT - 1),
                 waits=[tok_b, ("dve", prev_done),
                        ("pool", getattr(p, "k_const", 0))],
                 tick=(kd == NT - 1))
        tk_sum = p.tok("pe")
        tk_m = p.dve(CP(mrow, ps_s[0:1, :]), waits=[("pe", tk_sum)], tick=True)
        tk_rep = p.pe(MM(ps_r[:, :], B.ones_f[0:1, :], mrow, True, True),
                      waits=[("dve", tk_m)], tick=True)
        for kd in range(NT):
            p.dve(STT(t1[kd], ps_r[:, :], -1.0 / D, xf[kd], ALU.mult, ALU.add),
                  waits=[("pe", tk_rep), tok_f], tick=(kd == NT - 1))
        tk_t1 = p.tok("dve")
        for kd in range(NT):
            p.act(ACTV(sq[kd], t1[kd], AF.Square),
                  waits=[("dve", tk_t1), ("pe", tk_sum)], tick=(kd == NT - 1))
        tk_sq = p.tok("act")
        for kd in range(NT):
            p.pe(MM(ps_s[0:1, :], B.ones_b[:, 0:1], sq[kd], kd == 0, kd == NT - 1),
                 waits=[("act", tk_sq), ("dve", tk_m)], tick=(kd == NT - 1))
        tk_vs = p.tok("pe")
        tk_v = p.dve(TS2(vrow, ps_s[0:1, :], 1.0 / D, 1e-5, ALU.mult, ALU.add),
                     waits=[("pe", tk_vs)], tick=True)
        tk_sd = p.act(ACTV(sd, vrow, AF.Sqrt), waits=[("dve", tk_v)], tick=True)
        tk_rr = p.dve(RECIP(vrow, sd), waits=[("act", tk_sd)], tick=True)
        tk_rep2 = p.pe(MM(ps_r[:, :], B.ones_f[0:1, :], vrow, True, True),
                       waits=[("dve", tk_rr)], tick=True)
        for kd in range(NT):
            p.dve(TT(ysl(B, kd, t0, TC), t1[kd], ps_r[:, :], ALU.mult),
                  waits=[("pe", tk_rep2)], tick=(kd == NT - 1))
        prev_done = p.tok("dve")
    return prev_done


def emit_wo_residual(p, B, io, has_bo, tok_ctx):
    xin, xout, wo = io["xin"], io["xout"], io["wo"]
    wocb = [B.bf(A3 + 16384 + i * 4096, 2048) for i in range(2)]
    wo_guard = [p.tok("pe"), p.tok("pe")]
    bo_sb = io.get("bo_sb")
    xold = [B.ev[:, 0:512], B.ev[:, 512:1024]]
    xold_guard = [p.tok("dve"), p.tok("dve")]
    xnew = [B.ev[:, 1024:1536], B.evb.bitcast(F32)[:, 0:512]]
    xnew_guard = [(None, 0), (None, 0)]
    it = 0
    for m in range(NT):
        par = m % 2
        src = wo.ap()[:, m * 128:(m + 1) * 128].rearrange("(k p) m -> p k m", p=128)
        ld = p.dma1("pool", DMA(wocb[par][:].rearrange("p (k m) -> p k m", m=128),
                                src), waits=[("pe", wo_guard[par])])
        for tc in range(NCH):
            psp = B.ps[4 + (it % 2)]
            for kd in range(NT):
                wts = [ld, tok_ctx]
                if kd == 0:
                    wts.append(("dve", p.tok("dve")))
                p.pe(MM(psp[:, :], wocb[par][:, kd * 128:(kd + 1) * 128],
                        B.bf(A2 + kd * 2 * T + 2 * tc * TC, TC),
                        kd == 0, kd == NT - 1),
                     waits=wts, tick=(kd == NT - 1))
            tk_mm = p.tok("pe")
            wo_guard[par] = tk_mm
            xop = it % 2
            ldx = p.dma1("sp", DMA(xold[xop], xin.ap()[m * 128:(m + 1) * 128,
                                                       tc * TC:(tc + 1) * TC]),
                         waits=[("dve", xold_guard[xop])])
            bias = bo_sb[:, m:m + 1] if has_bo else 0.0
            w2 = [("pe", tk_mm), ldx]
            if xnew_guard[xop][0] is not None:
                w2.append(xnew_guard[xop])
            tk_add = p.dve(STT(xnew[xop], psp[:, :], bias, xold[xop],
                               ALU.add, ALU.add), waits=w2, tick=True)
            xold_guard[xop] = tk_add
            xnew_guard[xop] = p.dma1(
                "sp", DMA(xout.ap()[m * 128:(m + 1) * 128, tc * TC:(tc + 1) * TC],
                          xnew[xop]), waits=[("dve", tk_add)])
            it += 1


def emit_attn(p, B, io, cfg):
    H, dh, KVH = cfg["H"], cfg["dh"], cfg["KVH"]
    causal = cfg["causal"]
    rope = cfg.get("rope")
    has_b = cfg.get("bias", False)
    scale = 1.0 / float(np.sqrt(dh))
    WV = KVH * dh
    grp = H // KVH
    h2 = dh // 2
    gsa = cfg.get("gsa", False)
    NSL = 4 if gsa else 8
    mha = (grp == 1)
    VG = 512 // dh if mha else 0

    xin, xout = io["xin"], io["xout"]
    wq, wk, wv, wo, ctxh = io["wq"], io["wk"], io["wv"], io["wo"], io["ctxh"]

    setup_const(p, B)
    tk_msk = (None, 0)
    if causal:
        tk_msk = p.dma1("sp", DMA(B.msk[:, :], io["mask4"].ap()))
    tk_y = emit_ln(p, B, xin)

    V2 = [B.bf(A2 + i * 16384, NT * 512) for i in range(2)]
    Qb = [B.bf(A2 + 32768 + i * 4096, 2048)[0:dh, :] for i in range(2)]
    Kb = [B.bf(A2 + 40960 + i * 4096, 2048)[0:dh, :] for i in range(2)]
    wqcb2 = [B.bf(A2 + 49152 + i * 4096, NT * dh) for i in range(2)]
    wkcb2 = [B.bf(A2 + 57344 + i * 4096, NT * dh) for i in range(2)]
    wvj = [B.bf(A3 + 16384 + i * 4096, NT * 128) for i in range(2)]
    Pt = [B.bf(A3 + i * 1024, 512) for i in range(NSL)]
    rp1 = B.qk[0:dh, 0:2048]
    rp2 = B.qk[0:dh, 2048:4096]
    sqb = B.qk[0:dh, 4096:4608]
    sqb2 = B.qk[0:dh, 5154:5666]
    qss = B.srB[0:1, 0:2048]
    kss = B.srB[0:1, 2048:4096]
    ps_st4 = [B.ps[2], B.ps[3], B.ps[5], B.ps[6]]
    ps_st = [B.ps[2], B.ps[3]]
    ps_sum, ps_rep, ps_ctx, ps_pr = B.ps[0], B.ps[1], B.ps[4], B.ps[7]

    qmaxс = [B.colB[0:1, 0:1], B.colB[0:1, 1:2]]
    kmaxс = [B.colB[0:1, 2:3], B.colB[0:1, 3:4]]
    nbcell = [B.colB[0:1, 4:5], B.colB[0:1, 5:6]]
    nbcol = [B.colB[:, 6:7], B.colB[:, 7:8]]

    ropeC = ropeS = None
    tk_rope = (None, 0)
    if rope is not None:
        ropeC = B.bf(A3 + 8192, 2048)[0:dh, :]
        ropeS = B.bf(A3 + 12288, 2048)[0:dh, :]
        p.dma("pool", DMA(ropeC, io["ropeC"].ap()[0:dh, :]),
              waits=[("dve", tk_y)])
        p.dma("pool", DMA(ropeS, io["ropeS"].ap()[0:dh, :]))
        tk_rope = p.end_ep("pool")

    bq = io.get("bq_sb")
    bk = io.get("bk_sb")
    bvb = B.qk[0:1, 4642:5154]

    v_guard = [("pe", 0), ("pe", 0)]
    qk_guard = [("pe", 0), ("pe", 0)]
    wcb_guard = [("pe", 0), ("pe", 0)]
    wvj_guard = [("pe", 0), ("pe", 0)]
    kb_guard = [("pe", 0), ("pe", 0)]
    prod_done = {}
    tk_ctx_dma = [(None, 0), (None, 0)]
    ib_v = 0

    def emit_v_block(g, vpar):
        nonlocal ib_v
        tk_bv = (None, 0)
        if has_b:
            tk_bv = p.dma1("pool", DMA(bvb, io["bv"].ap()[0:1,
                                       g * 512:(g + 1) * 512]),
                           waits=[("pe", p.tok("pe"))])
        for jh in range(4):
            wp = ib_v % 2
            ib_v += 1
            for kd in range(NT):
                dst = wvj[wp][:, kd * 128:(kd + 1) * 128]
                srcap = wv.ap()[kd * 128:(kd + 1) * 128,
                                g * 512 + jh * 128:g * 512 + jh * 128 + 128]
                p.dma("pool", DMA(dst, srcap),
                      waits=[wvj_guard[wp], ("dve", tk_y),
                             ("act", p.tok("act"))])
            tok_wv = p.end_ep("pool")
            for m in range(NT):
                psv = ps_pr
                for kd in range(NT):
                    wts = [tok_wv]
                    if kd == 0:
                        wts.append(("dve", p.tok("dve")))
                        wts.append(v_guard[vpar])
                    last = (kd == NT - 1) and not has_b
                    p.pe(MM(psv[0:128, 0:128], ysl(B, kd, m * 128, 128),
                            wvj[wp][:, kd * 128:(kd + 1) * 128], kd == 0, last),
                         waits=wts, tick=last)
                if has_b:
                    p.pe(MM(psv[0:128, 0:128], B.ones_b[0:1, 0:128],
                            bvb[0:1, jh * 128:jh * 128 + 128], False, True),
                         waits=[tk_bv], tick=True)
                tk_mm = p.tok("pe")
                wvj_guard[wp] = ("pe", tk_mm)
                p.dve(CP(V2[vpar][:, m * 512 + jh * 128:m * 512 + jh * 128 + 128],
                         psv[0:128, 0:128]),
                      waits=[("pe", tk_mm)], tick=True)
        return p.tok("dve")

    def project_one(dst, wcb, w_dram, hcol, bias_ap, ld, stat_row, maxcell):
        roped = rope is not None
        tgt = rp1 if roped else dst
        ev_toks = []
        for tc in range(NCH):
            for kd in range(NT):
                wts = [ld]
                if kd == 0:
                    wts.append(("dve", p.tok("dve")))
                    wts.append(("act", p.tok("act")))
                p.pe(MM(ps_pr[0:dh, :], wcb[:, kd * dh:(kd + 1) * dh],
                        ysl(B, kd, tc * TC, TC), kd == 0, kd == NT - 1),
                     waits=wts, tick=(kd == NT - 1))
            tk_mm = p.tok("pe")
            b = bias_ap if bias_ap is not None else 0.0
            ev_toks.append(p.dve(TS1(tgt[:, tc * TC:(tc + 1) * TC],
                                     ps_pr[0:dh, :], b, ALU.add),
                                 waits=[("pe", tk_mm),
                                        io.get("bias_tok", (None, 0))],
                                 tick=True))
        tk_ev = ev_toks[-1]
        if roped:
            p.dve(CP(rp2[0:h2, :], rp1[h2:dh, :]),
                  waits=[("dve", tk_ev), tk_rope], tick=True)
            p.dve(CP(rp2[h2:dh, :], rp1[0:h2, :]), tick=True)
            for c4 in range(NCH):
                cs4 = slice(c4 * TC, (c4 + 1) * TC)
                p.dve(TT(sqb, rp2[:, cs4], ropeS[:, cs4], ALU.mult), tick=True)
                p.dve(TT(sqb2, rp1[:, cs4], ropeC[:, cs4], ALU.mult), tick=True)
                tk_ev = p.dve(TT(dst[:, cs4], sqb, sqb2, ALU.add), tick=True)
        red_toks = []
        for tc in range(NCH):
            tsq = p.dve(TT(sqb, dst[:, tc * TC:(tc + 1) * TC],
                           dst[:, tc * TC:(tc + 1) * TC], ALU.mult),
                        waits=[("dve", tk_ev), ("pe", p.tok("pe"))], tick=True)
            p.pe(MM(ps_pr[0:1, 0:512], B.ones_b[0:dh, 0:1], sqb,
                    True, True),
                 waits=[("dve", tsq), ("pool", p.k_const)], tick=True)
            red_toks.append(p.dve(CP(stat_row[0:1, tc * TC:(tc + 1) * TC],
                                     ps_pr[0:1, 0:512]),
                                  waits=[("pe", p.tok("pe"))], tick=True))
        tk_mx = p.dve(RMAX(maxcell, stat_row), waits=[("dve", red_toks[-1])],
                      tick=True)
        return tk_ev, tk_mx

    def produce(h):
        par = h % 2
        g = h // grp
        new_k = (h % grp == 0)
        srcq = wq.ap()[:, h * dh:(h + 1) * dh].rearrange("(k p) m -> p k m",
                                                         p=128)
        ldq = p.dma1("pool", DMA(
            wqcb2[par][:].rearrange("p (k m) -> p k m", m=dh), srcq),
            waits=[wcb_guard[par], ("dve", tk_y)])
        ldk = (None, 0)
        if new_k:
            srck = wk.ap()[:, g * dh:(g + 1) * dh].rearrange(
                "(k p) m -> p k m", p=128)
            ldk = p.dma1("pool", DMA(
                wkcb2[par][:].rearrange("p (k m) -> p k m", m=dh), srck),
                waits=[wcb_guard[par]])
        toks = {}
        if new_k:
            kpar = g % 2
            tk_kev, tk_kmx = project_one(
                Kb[kpar], wkcb2[par], wk, g,
                bk[0:dh, g:g + 1] if has_b else None, ldk, kss, kmaxс[kpar])
            toks["kev"], toks["kmx"] = tk_kev, tk_kmx
        tk_qev, tk_qmx = project_one(
            Qb[par], wqcb2[par], wq, h,
            bq[0:dh, h:h + 1] if has_b else None, ldq, qss, qmaxс[par])
        wcb_guard[par] = ("pe", p.tok("pe"))
        kpar = (h // grp) % 2
        kmx_tok = toks["kmx"] if new_k else prod_done[("kmx", h // grp)]
        tk_nb = p.pool(TS2(nbcell[par], qmaxс[par], kmaxс[kpar], -0.5 * scale,
                           ALU.add, ALU.mult),
                       waits=[("dve", tk_qmx), ("dve", kmx_tok)], tick=True)
        tk_nbr = p.pe(MM(ps_rep[:, 0:1], B.ones_f[0:1, :], nbcell[par],
                         True, True),
                      waits=[("pool", tk_nb), ("dve", p.tok("dve"))], tick=True)
        tk_nbc = p.dve(CP(nbcol[par], ps_rep[:, 0:1]), waits=[("pe", tk_nbr)],
                       tick=True)
        toks["qev"], toks["nbc"] = tk_qev, tk_nbc
        if new_k:
            prod_done[("kmx", h // grp)] = toks["kmx"]
        prod_done[h] = toks
        return toks

    if gsa:
        qa_sb = B.bf(A3 + 8192, 2048)[0:8, :]
        ka_sb = B.bf(A3 + 12288, 2048)[0:8, :]
        g0 = B.gsc[0:8, 0:512]
        g1 = B.gsc[0:8, 512:1024]
        g2 = B.gsc[0:8, 1024:1536]
        pos_nat = B.gsc[0:8, 1536:1664]
        dir_nat = B.gsc[0:8, 1664:1792]
        gtmp = B.gsc[0:8, 1792:1920]
        pn_nat = B.gsc[0:8, 1920:2048]
        pos_fm = B.ev[:, 1536:1544]
        dir_fm = B.ev[:, 1544:1552]
        posn_b = B.qk[:, 4608:4616]
        dirn_b = B.qk[:, 4616:4624]
        ampSka_b = B.qk[0:8, 4624:4625]
        possq_rb = B.qk[0:1, 4626:4634]
        negoff_rb = B.qk[0:1, 4634:4642]
        sc_c4 = B.colA[0:8, 4:5]
        sc_c5 = B.colA[0:8, 5:6]
        sc_c6 = B.colA[0:8, 6:7]
        possq_c = B.colA[0:8, 7:8]
        negoff_c = B.colA[0:8, 8:9]
        i2s2_c = B.colA[0:8, 9:10]
        ni2s2_c = B.colA[0:8, 10:11]
        amp_c = B.colA[0:8, 11:12]
        ds_c = B.colA[0:8, 12:13]
        row8a = B.srA[0:1, 768:776]
        row8b = B.srA[0:1, 776:784]
        obrow = B.srA[0:1, 512:640]
        blrow = B.srA[0:1, 640:768]
        winv_row = B.srA[0:1, 1024:1536]
        C0 = float(np.sqrt(128.0) * 0.3)

        def cell_chain(ops, first_waits=()):
            last = None
            for stream, fn in ops:
                w = [last] if last else list(first_waits)
                tk = getattr(p, stream)(fn, waits=w, tick=True)
                last = (stream, tk)
            return last

        def row_to_col(row_f32_1x8, col_dst, producer_tok, scale=1.0):
            tk_t = p.pe(MM(B.ps[1][0:8, 0:1], row_f32_1x8, B.ones_f[0:1, 0:1],
                           True, True),
                        waits=[producer_tok, ("dve", p.tok("dve")),
                               ("act", p.tok("act"))], tick=True)
            tk_e = p.act((lambda o=col_dst, s=scale: lambda e: e.mul(
                o, B.ps[1][0:8, 0:1], s))(), waits=[("pe", tk_t)], tick=True)
            return ("act", tk_e)

        def col_to_row(col_f32_8x1, row_dst, producer_tok, scale=1.0):
            tk_t = p.pe(MM(B.ps[1][0:1, 0:8], col_f32_8x1, id8_sb, True, True),
                        waits=[producer_tok, ("dve", p.tok("dve")),
                               ("act", p.tok("act"))], tick=True)
            tk_e = p.act((lambda o=row_dst, s=scale: lambda e: e.mul(
                o, B.ps[1][0:1, 0:8], s))(), waits=[("pe", tk_t)], tick=True)
            return ("act", tk_e)

        id8_sb = B.msk.bitcast(F32)[0:8, 0:8]
        tk_id = p.dma1("sp", DMA(id8_sb, io["id8"].ap()))
        tk_sc = p.dma1("sp", DMA(B.colA[0:1, 14:15], io["dstr"].ap()))
        tk_sc2 = p.dma1("sp", DMA(B.colA[0:1, 15:16], io["gstr"].ap()))
        tk_ds = p.act(ACTV(B.colA[0:1, 13:14], B.colA[0:1, 14:15], AF.Sigmoid),
                      waits=[tk_sc], tick=True)
        tk_gs = p.act(ACTV(B.colA[0:1, 14:15], B.colA[0:1, 15:16], AF.Sigmoid),
                      waits=[tk_sc2, ("act", tk_ds)], tick=True)
        tk_bl = p.dve(TS2(B.colA[0:1, 15:16], B.colA[0:1, 14:15], 0.1, 0.05,
                          ALU.mult, ALU.min), waits=[("act", tk_gs)], tick=True)
        tk_blr = p.dve(TS1(blrow, B.ones_f[0:1, :], B.colA[0:1, 15:16],
                           ALU.mult), waits=[("dve", tk_bl),
                                             ("pool", p.k_const)], tick=True)
        tk_ob = p.dve(TS2(B.colA[0:1, 14:15], B.colA[0:1, 15:16], -1.0, 1.0,
                          ALU.mult, ALU.add), waits=[("dve", tk_bl)], tick=True)
        tk_obr = p.dve(TS1(obrow, B.ones_f[0:1, :], B.colA[0:1, 14:15],
                           ALU.mult), waits=[("dve", tk_ob)], tick=True)
        tk_dsr = p.pe(MM(B.ps[1][0:8, 0:1], B.ones_f[0:1, 0:8],
                         B.colA[0:1, 13:14], True, True),
                      waits=[("act", tk_ds), ("dve", p.tok("dve"))], tick=True)
        tk_dsc = p.act((lambda: lambda e: e.copy(ds_c, B.ps[1][0:8, 0:1]))(),
                       waits=[("pe", tk_dsr)], tick=True)

        def gsa_prep_head(h, tk_headguard):
            hg = [tk_headguard, ("dve", p.tok("dve")), ("act", p.tok("act"))]
            e1 = p.dma1("sp", DMA(pos_nat, io["spos"].ap()[h, :, :]), waits=hg)
            e2 = p.dma1("sp", DMA(dir_nat, io["sdir"].ap()[h, :, :]), waits=hg)
            e3 = p.dma1("sp", DMA(pos_fm, io["sposT"].ap()[h, :, :]), waits=hg)
            e4 = p.dma1("sp", DMA(dir_fm, io["sdirT"].ap()[h, :, :]), waits=hg)
            e5 = p.dma1("sp", DMA(row8a, io["lsc"].ap()[h:h + 1, :]), waits=hg)
            e6 = p.dma1("sp", DMA(row8b, io["lam"].ap()[h:h + 1, :]), waits=hg)
            tk = cell_chain(first_waits=[e1], ops=[
                ("act", ACTV(gtmp, pos_nat, AF.Square)),
                ("dve", RSUM(sc_c4, gtmp)),
                ("act", ACTV(sc_c5, sc_c4, AF.Sqrt)),
                ("dve", TS1(sc_c4, sc_c5, 1e-12, ALU.add)),
                ("dve", RECIP(sc_c6, sc_c4)),
                ("dve", TT(sc_c4, sc_c5, sc_c6, ALU.mult)),
                ("dve", TS1(sc_c5, sc_c4, C0, ALU.mult)),
                ("dve", TT(possq_c, sc_c5, sc_c5, ALU.mult)),
                ("dve", TS1(sc_c5, sc_c6, C0, ALU.mult)),
                ("dve", TS1(pn_nat, pos_nat, sc_c5, ALU.mult)),
            ])
            tk_pn = tk
            tk_r2 = p.dve(TS1(sc_c4, sc_c5, -2.0, ALU.mult), waits=[tk_pn],
                          tick=True)
            r_row = col_to_row(sc_c4, B.srA[0:1, 784:792], ("dve", tk_r2))
            tk_rep = p.pe(MM(B.ps[1][:, 0:8], B.ones_f[0:1, :],
                             B.srA[0:1, 784:792], True, True),
                          waits=[r_row, ("dve", p.tok("dve"))], tick=True)
            tk_pb = p.dve(TT(posn_b, pos_fm, B.ps[1][:, 0:8], ALU.mult),
                          waits=[("pe", tk_rep), e3], tick=True)
            tk_d = cell_chain(first_waits=[e2, ("dve", p.tok("dve"))], ops=[
                ("act", ACTV(gtmp, dir_nat, AF.Square)),
                ("dve", RSUM(sc_c4, gtmp)),
                ("act", ACTV(sc_c5, sc_c4, AF.Sqrt)),
                ("dve", TS1(sc_c4, sc_c5, 1e-12, ALU.add)),
                ("dve", RECIP(sc_c5, sc_c4)),
                ("dve", TS1(gtmp, dir_nat, sc_c5, ALU.mult)),
            ])
            rd_row = col_to_row(sc_c5, B.srA[0:1, 792:800], tk_d)
            tk_rep2 = p.pe(MM(B.ps[1][:, 0:8], B.ones_f[0:1, :],
                              B.srA[0:1, 792:800], True, True),
                           waits=[rd_row, ("dve", tk_pb)], tick=True)
            tk_db = p.dve(TT(dirn_b, dir_fm, B.ps[1][:, 0:8], ALU.mult),
                          waits=[("pe", tk_rep2), e4], tick=True)
            tk_off = cell_chain([
                ("dve", TT(g0[:, 0:128], pn_nat, gtmp, ALU.mult)),
                ("dve", RSUM(sc_c4, g0[:, 0:128])),
                ("dve", TS1(negoff_c, sc_c4, -1.0, ALU.mult)),
            ])
            tk_s1 = p.act(ACTV(B.srA[0:1, 784:792], row8a, AF.Exp),
                          waits=[e5, ("pe", p.tok("pe")),
                                 ("dve", tk_off[1] if False else p.tok("dve"))],
                          tick=True)
            tk_s2 = p.dve(TS2(B.srA[0:1, 792:800], B.srA[0:1, 784:792],
                              1.2, 0.3, ALU.min, ALU.max),
                          waits=[("act", tk_s1)], tick=True)
            tk_s3 = p.pool(TT(B.srA[0:1, 784:792], B.srA[0:1, 792:800],
                              B.srA[0:1, 792:800], ALU.mult),
                           waits=[("dve", tk_s2)], tick=True)
            tk_s4 = p.dve(RECIP(B.srA[0:1, 792:800], B.srA[0:1, 784:792]),
                          waits=[("pool", tk_s3)], tick=True)
            tk_s5 = p.dve(TS1(B.srA[0:1, 784:792], B.srA[0:1, 792:800], 0.5,
                              ALU.mult), waits=[("dve", tk_s4)], tick=True)
            t_i2 = row_to_col(B.srA[0:1, 784:792], i2s2_c, ("dve", tk_s5))
            t_ni2 = cell_chain([
                ("dve", TS1(ni2s2_c, i2s2_c, -1.0, ALU.mult))],
                first_waits=[t_i2])
            tk_a1 = p.dve(RMAX(B.colA[0:1, 14:15], row8b, negate=True),
                          waits=[e6, t_i2], tick=True)
            tk_a2 = p.act(ACTV(B.srA[0:1, 784:792], row8b, AF.Exp,
                               bias=B.colA[0:1, 14:15]),
                          waits=[("dve", tk_a1)], tick=True)
            tk_a3 = p.dve(RSUM(B.colA[0:1, 14:15], B.srA[0:1, 784:792]),
                          waits=[("act", tk_a2)], tick=True)
            tk_a4 = p.dve(RECIP(B.colA[0:1, 15:16], B.colA[0:1, 14:15]),
                          waits=[("dve", tk_a3)], tick=True)
            tk_a5 = p.dve(TS1(B.srA[0:1, 792:800], B.srA[0:1, 784:792],
                              B.colA[0:1, 15:16], ALU.mult),
                          waits=[("dve", tk_a4)], tick=True)
            t_amp = row_to_col(B.srA[0:1, 792:800], amp_c, ("dve", tk_a5))
            tk_pr = p.act((lambda: lambda e: e.copy(
                possq_rb, B.ps[1][0:1, 0:8]))(), waits=[
                col_to_row(possq_c, B.srA[0:1, 784:792], tk_off),
            ], tick=True) if False else None
            r1 = col_to_row(possq_c, B.srA[0:1, 784:792], t_amp)
            tk_pq = p.dve(CP(possq_rb, B.srA[0:1, 784:792]), waits=[r1],
                          tick=True)
            r2 = col_to_row(negoff_c, B.srA[0:1, 792:800], ("dve", tk_pq))
            tk_no = p.dve(CP(negoff_rb, B.srA[0:1, 792:800]), waits=[r2],
                          tick=True)
            return {"pb": ("dve", tk_pb), "db": ("dve", tk_db),
                    "rows": ("dve", tk_no), "ds": ("act", tk_dsc)}

        def gsa_affinity(out_sb, x_ap, xsq_row, prep, fold_amp):
            sub = cfg.get("gsa_sub", "C")
            last = (None, 0)
            for c in range(NCH):
                cs = slice(c * TC, (c + 1) * TC)
                p.pe(MM(B.ps[2][0:8, :], posn_b, x_ap[:, cs], True,
                        sub == "A0"),
                     waits=[prep["pb"], prep["rows"],
                            ("dve", p.tok("dve")), ("act", p.tok("act"))],
                     tick=(sub == "A0"))
                if sub != "A0":
                    p.pe(MM(B.ps[2][0:8, :], B.ones_b[0:1, 0:8],
                            xsq_row[0:1, cs], False, False), waits=[])
                    p.pe(MM(B.ps[2][0:8, :], possq_rb, B.ones_b[0:1, 0:512],
                            False, True), waits=[], tick=True)
                tk_d2 = p.tok("pe")
                if sub in ("A0", "A"):
                    tx = p.dve(TS2(out_sb[:, cs], B.ps[2][0:8, :], 1e-3, 0.0,
                                   ALU.mult, ALU.max),
                               waits=[("pe", tk_d2), ("dve", p.tok("dve"))],
                               tick=True)
                    last = ("dve", tx)
                    continue
                p.pe(MM(B.ps[3][0:8, :], dirn_b, x_ap[:, cs], True, False),
                     waits=[prep["db"]])
                p.pe(MM(B.ps[3][0:8, :], negoff_rb, B.ones_b[0:1, 0:512],
                        False, True), waits=[], tick=True)
                tk_pj = p.tok("pe")
                ta = p.dve(TS2(g0, B.ps[2][0:8, :], ni2s2_c, 0.0,
                               ALU.mult, ALU.min),
                           waits=[("pe", tk_d2), ("dve", p.tok("dve"))],
                           tick=True)
                tb = p.act(ACTV(g1, B.ps[3][0:8, :], AF.Square),
                           waits=[("pe", tk_pj), ("act", p.tok("act"))],
                           tick=True)
                tc_ = p.dve(STT(g2, g1, i2s2_c, g0, ALU.mult, ALU.add),
                            waits=[("act", tb), ("dve", ta)], tick=True)
                td = p.dve(TS1(g1, g2, 0.0, ALU.min), waits=[("dve", tc_)],
                           tick=True)
                te = p.act(ACTV(g2, g0, AF.Exp), waits=[("dve", td)], tick=True)
                tf = p.act(ACTV(g0, g1, AF.Exp), waits=[("act", te)], tick=True)
                tg = p.dve(TT(g1, g0, g2, ALU.subtract), waits=[("act", tf)],
                           tick=True)
                if fold_amp:
                    th = p.dve(STT(g0, g1, ds_c, g2, ALU.mult, ALU.add),
                               waits=[("dve", tg), prep["ds"]], tick=True)
                    ti = p.dve(TS1(out_sb[:, cs], g0, amp_c, ALU.mult),
                               waits=[("dve", th)], tick=True)
                    last = ("dve", ti)
                else:
                    th = p.dve(STT(out_sb[:, cs], g1, ds_c, g2,
                                   ALU.mult, ALU.add),
                               waits=[("dve", tg), prep["ds"]], tick=True)
                    last = ("dve", th)
            return last


    if mha:
        emit_v_block(0, 0)
    else:
        for gg in range(1):
            pass
        emit_v_block(0, 0)
    produce(0)

    for h in range(H):
        par = h % 2
        g = h // grp
        kpar = g % 2
        vpar = (h // VG) % 2 if mha else 0
        vbase = (h % VG) * dh if mha else g * dh
        Q_ap = Qb[par]
        K_ap = Kb[kpar]
        toks = prod_done[h]
        tk_qstat = toks["qev"]
        tk_nbc = toks["nbc"]

        if gsa:
            prep = gsa_prep_head(h, ("pe", p.tok("pe")))
            tk_qa = gsa_affinity(qa_sb, Q_ap, qss, prep, False)
            tk_ka = gsa_affinity(ka_sb, K_ap, kss, prep, True)
            tk_ska = p.dve(RSUM(B.colA[0:8, 14:15], ka_sb),
                           waits=[tk_ka], tick=True)
            tk_skb = p.dve(CP(ampSka_b, B.colA[0:8, 14:15]),
                           waits=[("dve", tk_ska)], tick=True)

        bank_exp = [0, 0, 0, 0]
        slot_free = [("pe", 0)] * NSL
        LAG = 3

        def attn_tc(tc):
            nonlocal tk_ctx_dma
            ktmax = 4 * tc + 4 if causal else NT
            exp_tok = {}

            def emit_sums_av(j, last):
                sl = j % NSL
                wsm = [("act", exp_tok[j])]
                if j == 0:
                    wsm.append(("dve", p.tok("dve")))
                p.pe(MM(ps_sum[0:1, :], B.ones_b[:, 0:1], Pt[sl],
                        j == 0, last), waits=wsm, tick=last)
                p.pe(MM(ps_ctx[0:dh, :],
                        V2[vpar][:, j * 512 + vbase: j * 512 + vbase + dh],
                        Pt[sl], j == 0, last),
                     waits=[("act", exp_tok[j])] +
                           ([("dve", p.tok("dve")), v_guard[vpar] if False
                             else (None, 0)] if j == 0 else []),
                     tick=True)
                slot_free[sl] = ("pe", p.tok("pe"))

            for j in range(ktmax):
                sl = j % NSL
                bnk = j % 4
                psj = ps_st4[bnk]
                st_w = [("act", bank_exp[bnk]), ("dve", tk_qstat)]
                if bank_exp[bnk] == 0:
                    st_w.append(("dve", p.tok("dve")))
                p.pe(MM(psj[:, :], K_ap[:, j * 128:(j + 1) * 128],
                        Q_ap[:, tc * TC:(tc + 1) * TC], True, True),
                     waits=st_w, tick=True)
                dep = ("pe", p.tok("pe"))
                if causal and j >= 4 * tc:
                    u = j - 4 * tc
                    tk_mk = p.dve(TT(psj[:, :], psj[:, :],
                                     B.msk[:, u * 512:(u + 1) * 512],
                                     ALU.add),
                                  waits=[dep, tk_msk], tick=True)
                    dep = ("dve", tk_mk)
                bank_exp[bnk] = p.act(
                    ACTV(Pt[sl], psj[:, :], AF.Exp,
                         bias=nbcol[par], scale=scale),
                    waits=[dep, ("dve", tk_nbc), slot_free[sl]], tick=True)
                exp_tok[j] = bank_exp[bnk]
                if j >= LAG:
                    emit_sums_av(j - LAG, last=False)
            for j in range(max(0, ktmax - LAG), ktmax):
                emit_sums_av(j, last=(j == ktmax - 1))
            tk_avsum = p.tok("pe")
            inv_row = B.srA[0:1, 0:512]
            tk_inv = p.dve(RECIP(inv_row, ps_sum[0:1, :]),
                           waits=[("pe", tk_avsum)], tick=True)
            cpar = tc % 2
            ctx_st = B.evb[0:dh, cpar * 512:cpar * 512 + 512]
            if not gsa:
                tk_rp = p.pe(MM(ps_rep[:, :], B.ones_f[0:1, :], inv_row,
                                True, True),
                             waits=[("dve", tk_inv)], tick=True)
                tk_re = p.dve(CP(B.ev[:, 0:512], ps_rep[:, :]),
                              waits=[("pe", tk_rp)], tick=True)
                tk_cx = p.dve(TT(ctx_st, ps_ctx[0:dh, :], B.ev[0:dh, 0:512],
                                 ALU.mult),
                              waits=[("dve", tk_re), tk_ctx_dma[cpar]],
                              tick=True)
            else:
                tk_wsum = p.pe(MM(ps_sum[0:1, :], ampSka_b,
                                  qa_sb[:, tc * TC:(tc + 1) * TC],
                                  True, True),
                               waits=[("dve", tk_inv), ("dve", tk_skb)],
                               tick=True)
                bank_w = [("act", 0), ("act", 0)]
                Wt = [B.bf(A3 + 4096 + i * 1024, 512) for i in range(NSL)]
                for b0 in range(0, NT, NSL):
                    for j in range(b0, b0 + NSL):
                        sl = j - b0
                        bnk = j % 2
                        psj = ps_st[bnk]
                        p.pe(MM(psj[:, :], ka_sb[:, j * 128:(j + 1) * 128],
                                qa_sb[:, tc * TC:(tc + 1) * TC],
                                True, True),
                             waits=[bank_w[bnk]], tick=True)
                        bank_w[bnk] = ("act", p.act(
                            (lambda o=Wt[sl], i_=psj: lambda e: e.copy(
                                o, i_[:, :]))(),
                            waits=[("pe", p.tok("pe")),
                                   ("pe", p.tok("pe"))], tick=True))
                    tk_wexp = p.tok("act")
                    for j in range(b0, b0 + NSL):
                        p.pe(MM(B.ps[7][0:dh, :],
                                V2[vpar][:, j * 512 + vbase:
                                         j * 512 + vbase + dh],
                                Wt[j - b0], j == 0, j == NT - 1),
                             waits=[("act", tk_wexp),
                                    ("dve", p.tok("dve"))],
                             tick=(j == NT - 1))
                tk_gav = p.tok("pe")
                tk_w1 = p.dve(TS1(B.gsc[0:1, 0:512], ps_sum[0:1, :],
                                  1e-8, ALU.add),
                              waits=[("pe", tk_gav),
                                     ("dve", p.tok("dve"))], tick=True)
                tk_w2 = p.dve(RECIP(winv_row[0:1, 0:512],
                                    B.gsc[0:1, 0:512]),
                              waits=[("dve", tk_w1)], tick=True)
                tk_rp1 = p.pe(MM(ps_rep[:, :], obrow, inv_row, True, True),
                              waits=[("dve", tk_obr), ("dve", tk_w2),
                                     ("dve", p.tok("dve"))], tick=True)
                tk_re1 = p.dve(CP(B.ev[:, 0:512], ps_rep[:, :]),
                               waits=[("pe", tk_rp1)], tick=True)
                tk_t1 = p.dve(TT(B.ev[:, 512:1024], ps_ctx[:, :],
                                 B.ev[:, 0:512], ALU.mult),
                              waits=[("dve", tk_re1)], tick=True)
                tk_rp2 = p.pe(MM(ps_rep[:, :], blrow,
                                 winv_row[0:1, 0:512], True, True),
                              waits=[("dve", tk_blr), ("dve", tk_t1)],
                              tick=True)
                tk_re2 = p.dve(CP(B.ev[:, 0:512], ps_rep[:, :]),
                               waits=[("pe", tk_rp2)], tick=True)
                tk_t2 = p.dve(TT(B.ev[:, 1536:2048], B.ps[7][:, :],
                                 B.ev[:, 0:512], ALU.mult),
                              waits=[("dve", tk_re2)], tick=True)
                tk_cx = p.dve(TT(ctx_st, B.ev[:, 512:1024],
                                 B.ev[:, 1536:2048], ALU.add),
                              waits=[("dve", tk_t2), tk_ctx_dma[cpar]],
                              tick=True)

            tk_ctx_dma[cpar] = p.dma1("sp", DMA(
                ctxh.ap()[h * dh:(h + 1) * dh, tc * TC:(tc + 1) * TC],
                ctx_st), waits=[("dve", tk_cx)])

        attn_tc(0)
        attn_tc(1)
        if h + 1 < H:
            if mha and (h + 1) % VG == 0:
                nvpar = ((h + 1) // VG) % 2
                emit_v_block((h + 1) // VG, nvpar)
            produce(h + 1)
        attn_tc(2)
        attn_tc(3)
        v_guard[vpar] = ("pe", p.tok("pe"))
        qk_guard[par] = ("pe", p.tok("pe"))

    for kd in range(NT):
        p.dma("sp", DMA(B.bf(A2 + kd * 2 * T, T),
                        ctxh.ap()[kd * 128:(kd + 1) * 128, :]),
              waits=[("pe", p.tok("pe")), ("dve", p.tok("dve")),
                     tk_ctx_dma[0], tk_ctx_dma[1]])
    tok_ctx = p.end_ep("sp")
    emit_wo_residual(p, B, io, io.get("bo_sb") is not None, tok_ctx)


def emit_ffn0(p, B, io, do_quant=True):
    xin, xout = io["xin"], io["xout"]
    w1t, w2t = io["w1t"], io["w2t"]
    w1q, w2q = io["w1q"], io["w2q"]
    b1_sb, b2_sb = io["b1_sb"], io["b2_sb"]

    setup_const(p, B)
    tk_y = emit_ln(p, B, xin)

    qsl = [B.bf(A2 + i * 16384, 8192) for i in range(2)]
    fsl = [B.f32(A2 + 32768 + i * 8192, 2048) for i in range(2)]
    absw = B.f32(A2 + 49152, 2048)
    gew = B.f32(A2 + 57344, 2048)
    sgn = [B.bf(A3, 2048), B.bf(A3 + 12288, 2048)]
    qot = [B.bf(A3 + 4096 + i * 4096, 2048) for i in range(2)]
    redbuf = B.ev[:, 1024:1152]
    acc_col = B.colA[:, 4:5]
    acc_hbm = io["acc_hbm"]
    asum_row = B.srA[0:1, 0:128]
    cell = B.colA[0:1, 9:10]
    halfcell = B.colA[0:1, 10:11]
    shalf = B.colA[:, 11:12]
    store_toks = []
    wlist = (((w1t, w1q, 2048, 8192), (w2t, w2q, 8192, 2048))
             if do_quant else ())
    sl_guard = [("dve", tk_y), ("dve", tk_y)]
    f_guard = [("dve", 0), ("dve", 0)]
    q_guard = [(None, 0), (None, 0)]
    sgn_guard = [0, 0]
    absw_guard = ("dve", 0)
    for (wt, wq_s, nrow, ncol) in wlist:
        nel = nrow * ncol
        i = 0
        cw = min(8192, ncol)
        red_war = ("pool", p.tok("pool"))
        for r0 in range(0, nrow, 128):
            for c0 in range(0, ncol, cw):
                par = i % 2
                ld = p.dma1("pool", DMA(qsl[par][:, 0:cw],
                                        wt.ap()[r0:r0 + 128, c0:c0 + cw]),
                            waits=[sl_guard[par], ("act", p.tok("act"))])
                tk_r = p.dve(RSUM(redbuf[:, i:i + 1], qsl[par][:, 0:cw],
                                  absval=True),
                             waits=[ld, red_war], tick=True)
                sl_guard[par] = ("dve", tk_r)
                i += 1
        nslab = i
        tk_acc = p.dve(RSUM(acc_col, redbuf[:, 0:nslab]),
                       waits=[("dve", p.tok("dve"))], tick=True)
        tk_as = p.pool(RSUMC(B.srA[0:1, 128:129], acc_col),
                       waits=[("dve", tk_acc)], tick=True)
        tk_c = p.dve(TS2(cell, B.srA[0:1, 128:129], 1.0 / nel, 1e-8,
                         ALU.mult, ALU.max),
                     waits=[("pool", tk_as)], tick=True)
        tk_h = p.act((lambda: lambda e: e.mul(halfcell, cell, 0.5))(),
                     waits=[("dve", tk_c)], tick=True)
        tk_rp = p.pe(MM(B.ps[1][:, 0:1], B.ones_f[0:1, :], halfcell,
                        True, True),
                     waits=[("act", tk_h), ("dve", p.tok("dve"))], tick=True)
        tk_sh = p.act((lambda: lambda e: e.copy(shalf, B.ps[1][:, 0:1]))(),
                      waits=[("pe", tk_rp)], tick=True)
        red_war = ("pool", tk_c)
        i = 0
        last_store = (None, 0)
        for r0 in range(0, nrow, 128):
            for c0 in range(0, ncol, 2048):
                par = i % 2
                ld = p.dma1("pool", DMA(fsl[par], wt.ap()[r0:r0 + 128,
                                                          c0:c0 + 2048]),
                            waits=[f_guard[par]])
                ts = p.act(ACTV(sgn[par], fsl[par], AF.Sign),
                           waits=[ld, ("dve", sgn_guard[par])], tick=True)
                ta = p.act(ACTV(absw, fsl[par], AF.Abs),
                           waits=[ld, absw_guard], tick=True)
                f_guard[par] = ("act", ta)
                tg = p.dve(TS1(gew, absw, shalf, ALU.is_ge),
                           waits=[("act", tk_sh), ("act", ta)], tick=True)
                absw_guard = ("dve", tg)
                tq = p.dve(TT(qot[par], gew, sgn[par], ALU.mult),
                           waits=[("act", ts), q_guard[par]], tick=True)
                sgn_guard[par] = tq
                last_store = p.dma1("sp", DMA(
                    wq_s.ap()[r0:r0 + 128, c0:c0 + 2048], qot[par]),
                    waits=[("dve", tq)])
                q_guard[par] = last_store
                i += 1
        store_toks.append(last_store)
    if do_quant:
        tok_w1_done, tok_w2_done = store_toks
    else:
        tok_w1_done = tok_w2_done = (None, 0)

    h_guard = ("dve", p.tok("dve"))
    cb = [B.bf(A3 + i * 8192, 16 * 256) for i in range(3)]
    cb_guard = [("pe", p.tok("pe"))] * 3
    xold = [B.ev[:, 0:512], B.ev[:, 512:1024]]
    xold_guard = [p.tok("dve"), p.tok("dve")]
    xnew = [B.ev[:, 1024:1536], B.evb.bitcast(F32)[:, 0:512]]
    xnew_guard = [(None, 0), (None, 0)]
    h_evac_last = ("act", p.tok("act"))
    it = 0
    ib = 0
    for tc in range(NCH):
        for fg in range(32):
            par = ib % 3
            ib += 1
            ld = p.dma1("pool", DMA(
                cb[par][:].rearrange("p (k m) -> p k m", m=256),
                io["w1q"].ap()[:, fg * 256:(fg + 1) * 256].rearrange(
                    "(k p) m -> p k m", p=128)),
                waits=[cb_guard[par], tok_w1_done, h_guard])
            for mf in range(2):
                ft = fg * 2 + mf
                psp = B.ps[2 + (mf % 2)]
                for kd in range(NT):
                    wts = [ld]
                    if kd == 0:
                        wts.append(("act", p.tok("act")))
                    p.pe(MM(psp[:, :], cb[par][:, kd * 256 + mf * 128:
                                               kd * 256 + mf * 128 + 128],
                            ysl(B, kd, tc * TC, TC), kd == 0, kd == NT - 1),
                         waits=wts, tick=(kd == NT - 1))
                tk_mm = p.tok("pe")
                cb_guard[par] = ("pe", tk_mm)
                h_evac_last = ("act", p.act(
                    ACTV(B.bf(A2 + ft * 1024, 512), psp[:, :], AF.Gelu,
                         bias=b1_sb[:, ft:ft + 1]),
                    waits=[("pe", tk_mm), io["bias_tok"]], tick=True))
        for m in range(NT):
            psp = B.ps[5 + (m % 2)]
            for kh in range(2):
                par = ib % 3
                ib += 1
                ld = p.dma1("pool", DMA(
                    cb[par][:, 0:32 * 128].rearrange("p (k m) -> p k m", m=128),
                    io["w2q"].ap()[kh * 4096:(kh + 1) * 4096,
                                   m * 128:(m + 1) * 128].rearrange(
                        "(k p) m -> p k m", p=128)),
                    waits=[cb_guard[par], tok_w2_done])
                for fk in range(32):
                    fa = kh * 32 + fk
                    wts = [ld, h_evac_last]
                    if fa == 0:
                        wts.append(("dve", p.tok("dve")))
                    p.pe(MM(psp[:, :], cb[par][:, fk * 128:(fk + 1) * 128],
                            B.bf(A2 + fa * 1024, 512), fa == 0, fa == 63),
                         waits=wts, tick=(fa == 63 or fk == 31))
                cb_guard[par] = ("pe", p.tok("pe"))
            tk_mm = p.tok("pe")
            xop = it % 2
            ldx = p.dma1("sp", DMA(xold[xop], xin.ap()[m * 128:(m + 1) * 128,
                                                       tc * TC:(tc + 1) * TC]),
                         waits=[("dve", xold_guard[xop])])
            w2w = [("pe", tk_mm), ldx, io["bias_tok"]]
            if xnew_guard[xop][0] is not None:
                w2w.append(xnew_guard[xop])
            tk_add = p.dve(STT(xnew[xop], psp[:, :], b2_sb[:, m:m + 1],
                               xold[xop], ALU.add, ALU.add),
                           waits=w2w, tick=True)
            xold_guard[xop] = tk_add
            xnew_guard[xop] = p.dma1("sp", DMA(
                xout.ap()[m * 128:(m + 1) * 128, tc * TC:(tc + 1) * TC],
                xnew[xop]), waits=[("dve", tk_add)])
            it += 1
        h_guard = ("pe", p.tok("pe"))


def emit_moe(p, B, io):
    xin, xout = io["xin"], io["xout"]
    wg, wgu, wd = io["wg"], io["wgu"], io["wd"]
    wsg, wsd = io["wsg"], io["wsd"]
    routed_h = io["routed_h"]

    setup_const(p, B)
    tk_y = emit_ln(p, B, xin)

    wg_sb = B.bf(A4, 128)
    tk_wg = p.dma1("pool", DMA(wg_sb[:].rearrange("p (k e) -> p k e", e=8),
                               wg.ap().rearrange("(k p) e -> p k e", p=128)))
    id_sb = B.msk.bitcast(F32)[:, 0:128]
    tk_id = p.dma1("sp", DMA(id_sb, io["id128"].ap()))
    sel_tok = p.dma1("sp", DMA(B.sel[:, :], io["sel8"].ap()))

    lg = B.ev[:, 1024:1032]
    e1 = B.ev[:, 1040:1048]
    mk = B.ev[:, 1056:1064]
    e2 = B.ev[:, 1072:1080]
    c1 = B.ev[:, 1088:1096]
    comb = B.ev[:, 1104:1112]
    m1c, m2c, d12, w1c, w2c = (B.colA[:, 4:5], B.colA[:, 5:6], B.colA[:, 6:7],
                               B.colA[:, 7:8], B.colA[:, 8:9])
    for m in range(NT):
        psp = B.ps[4]
        for kd in range(NT):
            wts = [tk_wg, ("pool", p.k_const)]
            if kd == 0:
                wts.append(("dve", p.tok("dve")))
            p.pe(MM(psp[:, 0:8], ysl(B, kd, m * 128, 128),
                    wg_sb[:, kd * 8:(kd + 1) * 8], kd == 0, kd == NT - 1),
                 waits=wts, tick=(kd == NT - 1))
        tk_mm = p.tok("pe")
        t1_ = p.dve(CP(lg, psp[:, 0:8]), waits=[("pe", tk_mm)], tick=True)
        t2_ = p.dve(RMAX(m1c, lg), waits=[("dve", t1_)], tick=True)
        t3_ = p.dve(TS1(e1, lg, m1c, ALU.is_equal),
                    waits=[("dve", t2_)], tick=True)
        t4_ = p.dve(STT(mk, e1, -1e30, lg, ALU.mult, ALU.add),
                    waits=[("dve", t3_)], tick=True)
        t5_ = p.dve(RMAX(m2c, mk), waits=[("dve", t4_)], tick=True)
        t6_ = p.dve(TS1(e2, mk, m2c, ALU.is_equal), waits=[("dve", t5_)],
                    tick=True)
        t7_ = p.dve(TT(d12, m1c, m2c, ALU.subtract),
                    waits=[("dve", t6_)], tick=True)
        tk_s = p.act(ACTV(w1c, d12, AF.Sigmoid), waits=[("dve", t7_)],
                     tick=True)
        t8_ = p.dve(TS2(w2c, w1c, -1.0, 1.0, ALU.mult, ALU.add),
                    waits=[("act", tk_s)], tick=True)
        t9_ = p.dve(TS1(c1, e1, w1c, ALU.mult), waits=[("dve", t8_)],
                    tick=True)
        tk_cb = p.dve(STT(comb, e2, w2c, c1, ALU.mult, ALU.add),
                      waits=[("dve", t9_)], tick=True)
        tk_tr = p.pe((lambda cm=comb: lambda e: e.transpose(
            B.ps[7][0:8, 0:128], cm, id_sb))(),
            waits=[("dve", tk_cb), tk_id], tick=True)
        p.dve(CP(B.gsc[0:8, m * 128:(m + 1) * 128], B.ps[7][0:8, 0:128]),
              waits=[("pe", tk_tr)], tick=True)
    tk_comb = p.tok("dve")

    gub = [B.bf(A3 + i * 8192, 16 * 256) for i in range(3)]
    wdb = [B.bf(A2 + 32768 + 16384 + i * 2048, 1024) for i in range(2)]
    gbuf = [B.bf(A2 + 32768 + i * 1024, 512) for i in range(4)]
    crep = B.ev[:, 0:512]
    tmpf = B.ev[:, 512:1024]
    gu_guard = ("pe", p.tok("pe"))
    wd_guard = [("pe", 0), ("pe", 0)]
    act_guard = ("pe", p.tok("pe"))
    rout_st = [(None, 0), (None, 0)]
    xold = B.ev[:, 1024:1536]
    xold_guard = p.tok("dve")
    xnew = B.evb.bitcast(F32)[:, 0:512]
    xnew_guard = (None, 0)
    for tc in range(NCH):
        ib = getattr(p, "_moe_ib", 0)
        gu_guard3 = getattr(p, "_moe_gu3", [("pe", 0)] * 3)
        for e in range(8):
            tk_cr = p.pe(MM(B.ps[1][:, :], B.sel[:, e * 128:(e + 1) * 128],
                            B.gsc[0:8, tc * TC:(tc + 1) * TC], True, True),
                         waits=[("dve", tk_comb), sel_tok,
                                ("dve", p.tok("dve"))], tick=True)
            tk_ce = p.dve(CP(crep, B.ps[1][:, :]),
                          waits=[("pe", tk_cr), rout_st[0]], tick=True)
            for fg in range(8):
                par = ib % 3
                ib += 1
                ld = p.dma1("pool", DMA(
                    gub[par][:].rearrange("p (k m) -> p k m", m=256),
                    wgu.ap()[e, :, fg * 256:(fg + 1) * 256].rearrange(
                        "(k p) m -> p k m", p=128)),
                    waits=[gu_guard3[par], ("dve", tk_y)])
                for sub in range(2):
                    fi = fg * 2 + sub
                    psp = B.ps[2 + (sub % 2)]
                    for kd in range(NT):
                        wts = [ld]
                        if kd == 0:
                            wts.append(("act", p.tok("act")))
                            wts.append(("dve", p.tok("dve")))
                        p.pe(MM(psp[:, :], gub[par][:, kd * 256 + sub * 128:
                                                    kd * 256 + sub * 128 + 128],
                                ysl(B, kd, tc * TC, TC), kd == 0, kd == NT - 1),
                             waits=wts, tick=(kd == NT - 1))
                    tk_mm = p.tok("pe")
                    gu_guard3[par] = ("pe", tk_mm)
                    if fi < 8:
                        p.act(ACTV(B.bf(A2 + (e * 8 + fi) * 1024, 512),
                                   psp[:, :], AF.Silu),
                              waits=[("pe", tk_mm)], tick=True)
                    else:
                        g_sl = B.bf(A2 + (e * 8 + fi - 8) * 1024, 512)
                        tk_a1 = p.dve(TT(tmpf, psp[:, :], g_sl, ALU.mult),
                                      waits=[("pe", tk_mm),
                                             ("act", p.tok("act"))], tick=True)
                        p.dve(TT(g_sl, tmpf, crep, ALU.mult),
                              waits=[("dve", tk_a1)], tick=True)
        p._moe_ib, p._moe_gu3 = ib, gu_guard3
        act_done = p.tok("dve")
        if tc == 0 and "dbg_acts" in io:
            st0 = (None, 0)
            for c8 in range(8):
                p.dma("sp", DMA(io["dbg_acts"].ap()[:, c8 * 4096:(c8 + 1) * 4096],
                                B.bf(A2 + c8 * 8192, 4096)),
                      waits=[("dve", act_done)])
            io["dbg_acts_tok"] = p.end_ep("sp")
        wdall = [B.bf(A3 + i * 8192, 4096) for i in range(3)]
        ibd = getattr(p, "_moe_ibd", 0)
        wdall_guard = getattr(p, "_moe_wdg", [gu_guard3[0], gu_guard3[1],
                                              gu_guard3[2]])
        for hm in range(NT):
            psp = B.ps[5 + (hm % 2)]
            first = True
            for eh in range(2):
                par = ibd % 3
                ibd += 1
                ld = p.dma1("pool", DMA(
                    wdall[par][:].rearrange("p (e k m) -> p e k m", e=4, m=128),
                    wd.ap()[eh * 4:(eh + 1) * 4, :,
                            hm * 128:(hm + 1) * 128].rearrange(
                        "e (k p) m -> p e k m", p=128)),
                    waits=[wdall_guard[par]])
                for e4 in range(4):
                    e = eh * 4 + e4
                    for i in range(8):
                        wts = [ld, ("dve", act_done)]
                        if first:
                            wts.append(("dve", p.tok("dve")))
                        p.pe(MM(psp[:, :],
                                wdall[par][:, (e4 * 8 + i) * 128:
                                           (e4 * 8 + i) * 128 + 128],
                                B.bf(A2 + (e * 8 + i) * 1024, 512),
                                first, e == 7 and i == 7),
                             waits=wts, tick=(i == 7))
                        first = False
                wdall_guard[par] = ("pe", p.tok("pe"))
            tk_mm = p.tok("pe")
            rpar = hm % 2
            rslot = B.ev[:, 1536:2048] if rpar else B.ev[:, 0:512]
            tk_ev = p.dve(CP(rslot, psp[:, :]),
                          waits=[("pe", tk_mm), rout_st[rpar]], tick=True)
            rout_st[rpar] = p.dma1("sp", DMA(
                routed_h.ap()[hm * 128:(hm + 1) * 128, tc * TC:(tc + 1) * TC],
                rslot), waits=[("dve", tk_ev)])
        p._moe_ibd, p._moe_wdg = ibd, wdall_guard
        gu_guard3[0] = gu_guard3[1] = gu_guard3[2] = ("pe", p.tok("pe"))
        sh_guard = ("pe", p.tok("pe"))
        for fg in range(16):
            par = ib % 3
            ib += 1
            ldg = p.dma1("pool", DMA(
                gub[par][:].rearrange("p (k m) -> p k m", m=256),
                wsg.ap()[:, fg * 256:(fg + 1) * 256].rearrange(
                    "(k p) m -> p k m", p=128)),
                waits=[gu_guard3[par], sh_guard])
            silu_toks = []
            for sub in range(2):
                psp = B.ps[2 + (sub % 2)]
                for kd in range(NT):
                    wts = [ldg]
                    if kd == 0:
                        wts.append(("act", p.tok("act")))
                        wts.append(("dve", p.tok("dve")))
                    p.pe(MM(psp[:, :], gub[par][:, kd * 256 + sub * 128:
                                                kd * 256 + sub * 128 + 128],
                            ysl(B, kd, tc * TC, TC), kd == 0, kd == NT - 1),
                         waits=wts, tick=(kd == NT - 1))
                tk_mm = p.tok("pe")
                gu_guard3[par] = ("pe", tk_mm)
                silu_toks.append(p.act(ACTV(gbuf[sub], psp[:, :], AF.Silu),
                                       waits=[("pe", tk_mm),
                                              ("dve", p.tok("dve"))],
                                       tick=True))
            par = ib % 3
            ib += 1
            ldu = p.dma1("pool", DMA(
                gub[par][:].rearrange("p (k m) -> p k m", m=256),
                wsg.ap()[:, 4096 + fg * 256:4096 + (fg + 1) * 256].rearrange(
                    "(k p) m -> p k m", p=128)),
                waits=[gu_guard3[par]])
            for sub in range(2):
                it_i = fg * 2 + sub
                psp = B.ps[4 + (sub % 2)]
                for kd in range(NT):
                    wts = [ldu]
                    if kd == 0:
                        wts.append(("act", p.tok("act")))
                        wts.append(("dve", p.tok("dve")))
                    p.pe(MM(psp[:, :], gub[par][:, kd * 256 + sub * 128:
                                                kd * 256 + sub * 128 + 128],
                            ysl(B, kd, tc * TC, TC), kd == 0, kd == NT - 1),
                         waits=wts, tick=(kd == NT - 1))
                tk_mm = p.tok("pe")
                gu_guard3[par] = ("pe", tk_mm)
                p.dve(TT(B.bf(A2 + it_i * 1024, 512), psp[:, :], gbuf[sub],
                         ALU.mult),
                      waits=[("pe", tk_mm), ("act", silu_toks[sub])], tick=True)
        p._moe_ib, p._moe_gu3 = ib, gu_guard3
        actsh_done = p.tok("dve")
        shd = [B.bf(A2 + 36864, 32 * 128), B.bf(A2 + 53248, 32 * 128)]
        shd_guard = [wd_guard[0], wd_guard[1]]
        for hm in range(NT):
            par2 = hm % 2
            ld = p.dma1("pool", DMA(
                shd[par2][:].rearrange("p (k m) -> p k m", m=128),
                wsd.ap()[:, hm * 128:(hm + 1) * 128].rearrange(
                    "(k p) m -> p k m", p=128)),
                waits=[shd_guard[par2], ("pe", p.tok("pe")) if hm == 0
                       else (None, 0), ("dve", actsh_done)])
            psp = B.ps[5 + (hm % 2)]
            for i in range(32):
                wts = [ld, ("dve", actsh_done)]
                if i == 0:
                    wts.append(("dve", p.tok("dve")))
                p.pe(MM(psp[:, :],
                        shd[par2][:, i * 128:(i + 1) * 128],
                        B.bf(A2 + i * 1024, 512), i == 0, i == 31),
                     waits=wts, tick=(i == 31))
            shd_guard[par2] = ("pe", p.tok("pe"))
            tk_mm = p.tok("pe")
            wd_guard = [("pe", tk_mm), ("pe", tk_mm)]
            rpar = hm % 2
            rslot = B.ev[:, 1536:2048] if rpar else B.ev[:, 0:512]
            ldr = p.dma1("sp", DMA(
                rslot, routed_h.ap()[hm * 128:(hm + 1) * 128,
                                     tc * TC:(tc + 1) * TC]),
                waits=[rout_st[rpar], ("dve", p.tok("dve"))])
            rout_st[rpar] = ldr
            ldx = p.dma1("sp", DMA(xold, xin.ap()[hm * 128:(hm + 1) * 128,
                                                  tc * TC:(tc + 1) * TC]),
                         waits=[("dve", xold_guard)])
            tk_t = p.dve(TT(tmpf, psp[:, :], rslot, ALU.add),
                         waits=[("pe", tk_mm), ldr, ("dve", p.tok("dve"))],
                         tick=True)
            w2w = [("dve", tk_t), ldx]
            if xnew_guard[0] is not None:
                w2w.append(xnew_guard)
            tk_add = p.dve(TT(xnew, tmpf, xold, ALU.add), waits=w2w, tick=True)
            xold_guard = tk_add
            xnew_guard = p.dma1("sp", DMA(
                xout.ap()[hm * 128:(hm + 1) * 128, tc * TC:(tc + 1) * TC],
                xnew), waits=[("dve", tk_add)])


def np_bf16(a):
    return np.ascontiguousarray(np.asarray(a).astype(ml_dtypes.bfloat16))


def f32c(a):
    return np.ascontiguousarray(np.asarray(a, dtype=np.float32))


def make_mask4():
    r = np.arange(128)[:, None]
    c = np.arange(512)[None, :]
    m = np.zeros((128, 2048), np.float32)
    for u in range(4):
        m[:, u * 512:(u + 1) * 512] = np.where(r <= c - 128 * u, 0.0, -1e30)
    return np_bf16(m)


def rope_tables(dh, style):
    t = np.arange(T, dtype=np.float32)
    half = dh // 2
    if style == "half":
        inv = 1.0 / (10000.0 ** (np.arange(0, dh, 2, dtype=np.float32) / dh))
    else:
        inv = 1.0 / (10000.0 ** (np.arange(half, dtype=np.float32) / half))
    fr = t[:, None] * inv[None, :]
    C = np.concatenate([np.cos(fr).T, np.cos(fr).T], 0)
    S = np.concatenate([-np.sin(fr).T, np.sin(fr).T], 0)
    return f32c(C), f32c(S)


def deinterleave_cols(W, H, dh):
    perm = np.concatenate([np.arange(0, dh, 2), np.arange(1, dh, 2)])
    W4 = W.reshape(W.shape[0], H, dh)[:, :, perm]
    return f32c(W4.reshape(W.shape[0], H * dh))


_PROG_CACHE = {}


def build_attn_program(key, cfg):
    if key in _PROG_CACHE:
        return _PROG_CACHE[key]
    nc = bass.Bass("TRN2", target_bir_lowering=False)
    H, dh, KVH = cfg["H"], cfg["dh"], cfg["KVH"]
    WV = KVH * dh
    io = {}
    io["xin"] = nc.dram_tensor("xin", [D, T], F32, kind="ExternalInput")
    io["xout"] = nc.dram_tensor("xout", [D, T], F32, kind="ExternalOutput")
    io["wq"] = nc.dram_tensor("wq", [D, H * dh], F32, kind="ExternalInput")
    io["wk"] = nc.dram_tensor("wk", [D, WV], F32, kind="ExternalInput")
    io["wv"] = nc.dram_tensor("wv", [D, WV], F32, kind="ExternalInput")
    io["wo"] = nc.dram_tensor("wo", [H * dh, D], F32, kind="ExternalInput")
    io["ctxh"] = nc.dram_tensor("ctxh", [H * dh, T], BF16, kind="Internal")
    if cfg["causal"]:
        io["mask4"] = nc.dram_tensor("mask4", [128, 2048], BF16,
                                     kind="ExternalInput")
    if cfg.get("rope"):
        io["ropeC"] = nc.dram_tensor("ropeC", [dh, T], F32, kind="ExternalInput")
        io["ropeS"] = nc.dram_tensor("ropeS", [dh, T], F32, kind="ExternalInput")
    if cfg.get("gsa"):
        io["spos"] = nc.dram_tensor("spos", [16, 8, 128], F32, kind="ExternalInput")
        io["sdir"] = nc.dram_tensor("sdir", [16, 8, 128], F32, kind="ExternalInput")
        io["sposT"] = nc.dram_tensor("sposT", [16, 128, 8], F32, kind="ExternalInput")
        io["sdirT"] = nc.dram_tensor("sdirT", [16, 128, 8], F32, kind="ExternalInput")
        io["lsc"] = nc.dram_tensor("lsc", [16, 8], F32, kind="ExternalInput")
        io["lam"] = nc.dram_tensor("lam", [16, 8], F32, kind="ExternalInput")
        io["dstr"] = nc.dram_tensor("dstr", [1, 1], F32, kind="ExternalInput")
        io["gstr"] = nc.dram_tensor("gstr", [1, 1], F32, kind="ExternalInput")
        io["id8"] = nc.dram_tensor("id8", [8, 8], F32, kind="ExternalInput")
    with contextlib.ExitStack() as stack:
        p = Plan()
        B = Bufs(nc, stack)
        if cfg.get("bias"):
            bq_d = nc.dram_tensor("bq", [dh, H], F32, kind="ExternalInput")
            bk_d = nc.dram_tensor("bk", [dh, KVH], F32, kind="ExternalInput")
            bv_d = nc.dram_tensor("bv", [1, WV], F32, kind="ExternalInput")
            bo_d = nc.dram_tensor("bo", [128, 16], F32, kind="ExternalInput")
            bq_sb = stack.enter_context(nc.sbuf_tensor("bq_sb", [dh, H], F32))
            bk_sb = stack.enter_context(nc.sbuf_tensor("bk_sb", [dh, KVH], F32))
            bo_sb = stack.enter_context(nc.sbuf_tensor("bo_sb", [128, 16], F32))
            p.dma("sp", DMA(bq_sb[:, :], bq_d.ap()))
            p.dma("sp", DMA(bk_sb[:, :], bk_d.ap()))
            p.dma("sp", DMA(bo_sb[:, :], bo_d.ap()))
            io["bias_tok"] = p.end_ep("sp")
            io.update(bv=bv_d, bq_sb=bq_sb, bk_sb=bk_sb, bo_sb=bo_sb)
        emit_attn(p, B, io, cfg)
        p.emit(nc)
    _PROG_CACHE[key] = nc
    return nc


def attn_host_inputs(cfg, x_fm, Wq, Wk, Wv, Wo, bq=None, bk=None, bv=None,
                     bo=None, pre_arranged=False):
    H, dh, KVH = cfg["H"], cfg["dh"], cfg["KVH"]
    m = {"xin": x_fm, "wq": f32c(Wq), "wk": f32c(Wk), "wv": f32c(Wv),
         "wo": f32c(Wo)}
    if cfg["causal"]:
        m["mask4"] = make_mask4()
    if cfg.get("rope"):
        C, S = rope_tables(dh, cfg["rope"])
        m["ropeC"], m["ropeS"] = C, S
    if cfg.get("bias"):
        m["bq"] = f32c(np.asarray(bq).reshape(H, dh).T)
        m["bk"] = f32c(np.asarray(bk).reshape(KVH, dh).T)
        m["bv"] = f32c(np.asarray(bv).reshape(1, KVH * dh))
        m["bo"] = f32c(np.asarray(bo).reshape(16, 128).T)
    return m


def gsa_host_inputs(spos, sdir, lsc, lam, dstr, gstr):
    return {"spos": f32c(spos), "sdir": f32c(sdir),
            "sposT": f32c(np.asarray(spos).transpose(0, 2, 1)),
            "sdirT": f32c(np.asarray(sdir).transpose(0, 2, 1)),
            "lsc": f32c(lsc), "lam": f32c(lam),
            "dstr": f32c(np.asarray(dstr).reshape(1, 1)),
            "gstr": f32c(np.asarray(gstr).reshape(1, 1)),
            "id8": np.eye(8, dtype=np.float32)}




def build_ffn_program():
    if "ffn0" in _PROG_CACHE:
        return _PROG_CACHE["ffn0"]
    nc = bass.Bass("TRN2", target_bir_lowering=False)
    io = {}
    io["xin"] = nc.dram_tensor("xin", [D, T], F32, kind="ExternalInput")
    io["xout"] = nc.dram_tensor("xout", [D, T], F32, kind="ExternalOutput")
    io["w1t"] = nc.dram_tensor("w1t", [2048, 8192], F32, kind="ExternalInput")
    io["w2t"] = nc.dram_tensor("w2t", [8192, 2048], F32, kind="ExternalInput")
    io["w1q"] = nc.dram_tensor("w1q", [2048, 8192], BF16, kind="Internal")
    io["w2q"] = nc.dram_tensor("w2q", [8192, 2048], BF16, kind="Internal")
    io["acc_hbm"] = nc.dram_tensor("acc_hbm", [1, 128], F32, kind="Internal")
    b1_d = nc.dram_tensor("b1", [128, 64], F32, kind="ExternalInput")
    b2_d = nc.dram_tensor("b2", [128, 16], F32, kind="ExternalInput")
    with contextlib.ExitStack() as stack:
        p = Plan()
        B = Bufs(nc, stack)
        b1_sb = stack.enter_context(nc.sbuf_tensor("b1_sb", [128, 64], F32))
        b2_sb = stack.enter_context(nc.sbuf_tensor("b2_sb", [128, 16], F32))
        p.dma("sp", DMA(b1_sb[:, :], b1_d.ap()))
        p.dma("sp", DMA(b2_sb[:, :], b2_d.ap()))
        io["bias_tok"] = p.end_ep("sp")
        io["b1_sb"], io["b2_sb"] = b1_sb, b2_sb
        emit_ffn0(p, B, io)
        p.emit(nc)
    _PROG_CACHE["ffn0"] = nc
    return nc


def build_moe_program():
    if "moe" in _PROG_CACHE:
        return _PROG_CACHE["moe"]
    nc = bass.Bass("TRN2", target_bir_lowering=False)
    io = {}
    io["xin"] = nc.dram_tensor("xin", [D, T], F32, kind="ExternalInput")
    io["xout"] = nc.dram_tensor("xout", [D, T], F32, kind="ExternalOutput")
    io["wg"] = nc.dram_tensor("wg", [2048, 8], F32, kind="ExternalInput")
    io["wgu"] = nc.dram_tensor("wgu", [8, 2048, 2048], F32, kind="ExternalInput")
    io["wd"] = nc.dram_tensor("wd", [8, 1024, 2048], F32, kind="ExternalInput")
    io["wsg"] = nc.dram_tensor("wsg", [2048, 8192], F32, kind="ExternalInput")
    io["wsd"] = nc.dram_tensor("wsd", [4096, 2048], F32, kind="ExternalInput")
    io["id128"] = nc.dram_tensor("id128", [128, 128], F32, kind="ExternalInput")
    io["sel8"] = nc.dram_tensor("sel8", [8, 1024], F32, kind="ExternalInput")
    io["routed_h"] = nc.dram_tensor("routed_h", [D, T], F32, kind="Internal")
    with contextlib.ExitStack() as stack:
        p = Plan()
        B = Bufs(nc, stack)
        emit_moe(p, B, io)
        p.emit(nc)
    _PROG_CACHE["moe"] = nc
    return nc


def moe_host_inputs(x_fm, wg, wgu, wd, wsg, wsd):
    sel = np.zeros((8, 1024), np.float32)
    for e in range(8):
        sel[e, e * 128:(e + 1) * 128] = 1.0
    return {"xin": x_fm, "wg": f32c(wg), "wgu": f32c(wgu), "wd": f32c(wd),
            "wsg": f32c(wsg), "wsd": f32c(wsd),
            "id128": np.eye(128, dtype=np.float32), "sel8": sel}


def ffn_host_inputs(x_fm, w1, b1, w2, b2):
    return {"xin": x_fm, "w1t": f32c(np.asarray(w1).T),
            "w2t": f32c(np.asarray(w2).T),
            "b1": f32c(np.asarray(b1).reshape(64, 128).T),
            "b2": f32c(np.asarray(b2).reshape(16, 128).T)}


def run_spmd(nc, in_maps):
    res = run_bass_kernel_spmd(nc, in_maps, core_ids=list(range(8)))
    return res.results


def build_fused_program():
    if "fused" in _PROG_CACHE:
        return _PROG_CACHE["fused"]
    nc = bass.Bass("TRN2", target_bir_lowering=False)
    xb = [nc.dram_tensor("xin", [D, T], F32, kind="ExternalInput")]
    for i in range(1, 8):
        xb.append(nc.dram_tensor("x%d" % i, [D, T], F32, kind="Internal"))
    xb.append(nc.dram_tensor("xout", [D, T], F32, kind="ExternalOutput"))
    ctxh = nc.dram_tensor("ctxh", [2048, T], BF16, kind="Internal")

    def dt(name, shape):
        return nc.dram_tensor(name, shape, F32, kind="ExternalInput")

    io_a = {"wq": dt("a_wq", [D, D]), "wk": dt("a_wk", [D, D]),
            "wv": dt("a_wv", [D, D]), "wo": dt("a_wo", [D, D]),
            "spos": dt("spos", [16, 8, 128]), "sdir": dt("sdir", [16, 8, 128]),
            "sposT": dt("sposT", [16, 128, 8]), "sdirT": dt("sdirT", [16, 128, 8]),
            "lsc": dt("lsc", [16, 8]), "lam": dt("lam", [16, 8]),
            "dstr": dt("dstr", [1, 1]), "gstr": dt("gstr", [1, 1]),
            "id8": dt("id8", [8, 8]), "ctxh": ctxh}
    io_f = {"w1t": dt("f_w1t", [2048, 8192]), "w2t": dt("f_w2t", [8192, 2048]),
            "w1q": nc.dram_tensor("f_w1q", [2048, 8192], BF16, kind="Internal"),
            "w2q": nc.dram_tensor("f_w2q", [8192, 2048], BF16, kind="Internal"),
            "acc_hbm": nc.dram_tensor("f_acc", [1, 128], F32, kind="Internal")}
    io_b = {"wq": dt("b_wq", [D, D]), "wk": dt("b_wk", [D, D]),
            "wv": dt("b_wv", [D, D]), "wo": dt("b_wo", [D, D]),
            "mask4": nc.dram_tensor("mask4", [128, 2048], BF16,
                                    kind="ExternalInput"),
            "ropeC": dt("b_ropeC", [128, T]), "ropeS": dt("b_ropeS", [128, T]),
            "ctxh": ctxh}
    io_c = {"wq": dt("c_wq", [D, D]), "wk": dt("c_wk", [D, D]),
            "wv": dt("c_wv", [D, D]), "wo": dt("c_wo", [D, D]),
            "mask4": io_b["mask4"], "ctxh": ctxh}
    io_m = {"wg": dt("m_wg", [2048, 8]), "wgu": dt("m_wgu", [8, 2048, 2048]),
            "wd": dt("m_wd", [8, 1024, 2048]), "wsg": dt("m_wsg", [2048, 8192]),
            "wsd": dt("m_wsd", [4096, 2048]), "id128": dt("id128", [128, 128]),
            "sel8": dt("sel8", [8, 1024]),
            "routed_h": nc.dram_tensor("m_routed", [D, T], F32, kind="Internal")}
    io_d = {"wq": dt("d_wq", [D, D]), "wk": dt("d_wk", [D, 512]),
            "wv": dt("d_wv", [D, 512]), "wo": dt("d_wo", [D, D]),
            "mask4": io_b["mask4"],
            "ropeC": dt("d_ropeC", [64, T]), "ropeS": dt("d_ropeS", [64, T]),
            "ctxh": ctxh}
    with contextlib.ExitStack() as stack:
        p = Plan()
        B = Bufs(nc, stack)
        bq_b = dt("b_bq", [128, 16]); bk_b = dt("b_bk", [128, 16])
        bv_b = dt("b_bv", [1, 2048]); bo_b = dt("b_bo", [128, 16])
        bq_c = dt("c_bq", [128, 16]); bk_c = dt("c_bk", [128, 16])
        bv_c = dt("c_bv", [1, 2048]); bo_c = dt("c_bo", [128, 16])
        b1_d = dt("f_b1", [128, 64]); b2_d = dt("f_b2", [128, 16])
        bq_sb = stack.enter_context(nc.sbuf_tensor("bq_sb", [128, 16], F32))
        bk_sb = stack.enter_context(nc.sbuf_tensor("bk_sb", [128, 16], F32))
        bo_sb = stack.enter_context(nc.sbuf_tensor("bo_sb", [128, 16], F32))
        b1_sb = stack.enter_context(nc.sbuf_tensor("b1_sb", [128, 64], F32))
        b2_sb = stack.enter_context(nc.sbuf_tensor("b2_sb", [128, 16], F32))
        io_f["b1_sb"], io_f["b2_sb"] = b1_sb, b2_sb

        def load_bias(io, bq_d, bk_d, bv_d, bo_d):
            p.dma("sp", DMA(bq_sb[:, :], bq_d.ap()))
            p.dma("sp", DMA(bk_sb[:, :], bk_d.ap()))
            p.dma("sp", DMA(bo_sb[:, :], bo_d.ap()))
            io["bias_tok"] = p.end_ep("sp")
            io.update(bv=bv_d, bq_sb=bq_sb, bk_sb=bk_sb, bo_sb=bo_sb)

        io_a.update(xin=xb[0], xout=xb[1])
        emit_attn(p, B, io_a, CFG_A)
        p.set_barrier()
        io_f.update(xin=xb[1], xout=xb[2])
        p.dma("sp", DMA(b1_sb[:, :], b1_d.ap()))
        p.dma("sp", DMA(b2_sb[:, :], b2_d.ap()))
        io_f["bias_tok"] = p.end_ep("sp")
        emit_ffn0(p, B, io_f, do_quant=True)
        p.set_barrier()
        io_b.update(xin=xb[2], xout=xb[3])
        load_bias(io_b, bq_b, bk_b, bv_b, bo_b)
        emit_attn(p, B, io_b, CFG_B)
        p.set_barrier()
        io_f2 = dict(io_f, xin=xb[3], xout=xb[4])
        emit_ffn0(p, B, io_f2, do_quant=False)
        p.set_barrier()
        io_c.update(xin=xb[4], xout=xb[5])
        load_bias(io_c, bq_c, bk_c, bv_c, bo_c)
        emit_attn(p, B, io_c, CFG_C)
        p.set_barrier()
        io_m.update(xin=xb[5], xout=xb[6])
        emit_moe(p, B, io_m)
        p.set_barrier()
        io_d.update(xin=xb[6], xout=xb[7])
        emit_attn(p, B, io_d, CFG_D)
        p.set_barrier()
        io_m2 = dict(io_m, xin=xb[7], xout=xb[8])
        emit_moe(p, B, io_m2)
        p.emit(nc)
    _PROG_CACHE["fused"] = nc
    return nc


def fused_host_inputs(inputs, x_fm):
    f = lambda k: f32c(inputs[k])
    m = {"xin": x_fm}
    m.update({"a_wq": f("L0_attn_a_Wq"), "a_wk": f("L0_attn_a_Wk"),
              "a_wv": f("L0_attn_a_Wv"), "a_wo": f("L0_attn_a_Wo")})
    g = gsa_host_inputs(f("L0_attn_a_splat_positions"),
                        f("L0_attn_a_splat_directions"),
                        f("L0_attn_a_splat_log_scales"),
                        f("L0_attn_a_splat_log_amplitudes"),
                        f("L0_attn_a_directional_strength"),
                        f("L0_attn_a_gsa_strength"))
    m.update(g)
    m.update({"f_w1t": f32c(f("L0_ffn_w1").T), "f_w2t": f32c(f("L0_ffn_w2").T),
              "f_b1": f32c(np.asarray(inputs["L0_ffn_b1"]).reshape(64, 128).T),
              "f_b2": f32c(np.asarray(inputs["L0_ffn_b2"]).reshape(16, 128).T)})
    C, S = rope_tables(128, "half")
    m.update({"b_wq": f32c(f("L0_attn_b_Wq").T), "b_wk": f32c(f("L0_attn_b_Wk").T),
              "b_wv": f32c(f("L0_attn_b_Wv").T), "b_wo": f32c(f("L0_attn_b_Wo").T),
              "b_ropeC": C, "b_ropeS": S, "mask4": make_mask4(),
              "b_bq": f32c(np.asarray(inputs["L0_attn_b_bq"]).reshape(16, 128).T),
              "b_bk": f32c(np.asarray(inputs["L0_attn_b_bk"]).reshape(16, 128).T),
              "b_bv": f32c(np.asarray(inputs["L0_attn_b_bv"]).reshape(1, 2048)),
              "b_bo": f32c(np.asarray(inputs["L0_attn_b_bo"]).reshape(16, 128).T)})
    m.update({"c_wq": f32c(f("L1_attn_a_Wq").T), "c_wk": f32c(f("L1_attn_a_Wk").T),
              "c_wv": f32c(f("L1_attn_a_Wv").T), "c_wo": f32c(f("L1_attn_a_Wo").T),
              "c_bq": f32c(np.asarray(inputs["L1_attn_a_bq"]).reshape(16, 128).T),
              "c_bk": f32c(np.asarray(inputs["L1_attn_a_bk"]).reshape(16, 128).T),
              "c_bv": f32c(np.asarray(inputs["L1_attn_a_bv"]).reshape(1, 2048)),
              "c_bo": f32c(np.asarray(inputs["L1_attn_a_bo"]).reshape(16, 128).T)})
    mm = moe_host_inputs(None, f("L1_ffn_w_gate"), f("L1_ffn_w_gate_up"),
                         f("L1_ffn_w_down"), f("L1_ffn_w_sh_gate_up"),
                         f("L1_ffn_w_sh_down"))
    del mm["xin"]
    m.update({"m_wg": mm["wg"], "m_wgu": mm["wgu"], "m_wd": mm["wd"],
              "m_wsg": mm["wsg"], "m_wsd": mm["wsd"], "id128": mm["id128"],
              "sel8": mm["sel8"]})
    C2, S2 = rope_tables(64, "ileave")
    m.update({"d_wq": deinterleave_cols(f("L1_attn_b_Wq"), 32, 64),
              "d_wk": deinterleave_cols(f("L1_attn_b_Wk"), 8, 64),
              "d_wv": f("L1_attn_b_Wv"), "d_wo": f("L1_attn_b_Wo"),
              "d_ropeC": C2, "d_ropeS": S2})
    return m


def kernel_fused(**inputs):
    x0 = np.asarray(inputs["L0_attn_a_hidden_states"], dtype=np.float32)
    nc = build_fused_program()
    in_maps = [fused_host_inputs(inputs, np.ascontiguousarray(x0[c].T))
               for c in range(8)]
    res = run_spmd(nc, in_maps)
    return np.ascontiguousarray(
        np.stack([np.asarray(res[c]["xout"]).T for c in range(8)],
                 0)).astype(np.float32)


CFG_A = dict(H=16, dh=128, KVH=16, causal=False, rope=None, bias=False, gsa=True)
CFG_B = dict(H=16, dh=128, KVH=16, causal=True, rope="half", bias=True)
CFG_C = dict(H=16, dh=128, KVH=16, causal=True, rope=None, bias=True)
CFG_D = dict(H=32, dh=64, KVH=8, causal=True, rope="ileave", bias=False)


def kernel(**inputs):
    return kernel_fused(**inputs)


def kernel_multilaunch(**inputs):
    f = lambda k: np.asarray(inputs[k], dtype=np.float32)
    x0 = f("L0_attn_a_hidden_states")
    xs = [np.ascontiguousarray(x0[c].T) for c in range(8)]

    nc_a = build_attn_program("attn_a", CFG_A)
    nc_f = build_ffn_program()
    nc_b = build_attn_program("attn_b", CFG_B)
    nc_c = build_attn_program("attn_c", CFG_C)
    nc_m = build_moe_program()
    nc_d = build_attn_program("attn_d", CFG_D)

    def step(nc, maps):
        res = run_spmd(nc, maps)
        return [np.asarray(res[c]["xout"]) for c in range(8)]

    gsa_m = gsa_host_inputs(f("L0_attn_a_splat_positions"),
                            f("L0_attn_a_splat_directions"),
                            f("L0_attn_a_splat_log_scales"),
                            f("L0_attn_a_splat_log_amplitudes"),
                            f("L0_attn_a_directional_strength"),
                            f("L0_attn_a_gsa_strength"))
    maps = []
    for c in range(8):
        m = attn_host_inputs(CFG_A, xs[c], f("L0_attn_a_Wq"), f("L0_attn_a_Wk"),
                             f("L0_attn_a_Wv"), f("L0_attn_a_Wo"))
        m.update(gsa_m)
        maps.append(m)
    xs = step(nc_a, maps)

    ffn_m = ffn_host_inputs(None, f("L0_ffn_w1"), f("L0_ffn_b1"),
                            f("L0_ffn_w2"), f("L0_ffn_b2"))
    del ffn_m["xin"]
    xs = step(nc_f, [dict(ffn_m, xin=xs[c]) for c in range(8)])

    maps = [attn_host_inputs(CFG_B, xs[c], f("L0_attn_b_Wq").T,
                             f("L0_attn_b_Wk").T, f("L0_attn_b_Wv").T,
                             f("L0_attn_b_Wo").T, bq=f("L0_attn_b_bq"),
                             bk=f("L0_attn_b_bk"), bv=f("L0_attn_b_bv"),
                             bo=f("L0_attn_b_bo")) for c in range(8)]
    xs = step(nc_b, maps)

    xs = step(nc_f, [dict(ffn_m, xin=xs[c]) for c in range(8)])

    maps = [attn_host_inputs(CFG_C, xs[c], f("L1_attn_a_Wq").T,
                             f("L1_attn_a_Wk").T, f("L1_attn_a_Wv").T,
                             f("L1_attn_a_Wo").T, bq=f("L1_attn_a_bq"),
                             bk=f("L1_attn_a_bk"), bv=f("L1_attn_a_bv"),
                             bo=f("L1_attn_a_bo")) for c in range(8)]
    xs = step(nc_c, maps)

    moe_m = moe_host_inputs(None, f("L1_ffn_w_gate"), f("L1_ffn_w_gate_up"),
                            f("L1_ffn_w_down"), f("L1_ffn_w_sh_gate_up"),
                            f("L1_ffn_w_sh_down"))
    del moe_m["xin"]
    xs = step(nc_m, [dict(moe_m, xin=xs[c]) for c in range(8)])

    wq_p = deinterleave_cols(f("L1_attn_b_Wq"), 32, 64)
    wk_p = deinterleave_cols(f("L1_attn_b_Wk"), 8, 64)
    maps = [attn_host_inputs(CFG_D, xs[c], wq_p, wk_p, f("L1_attn_b_Wv"),
                             f("L1_attn_b_Wo")) for c in range(8)]
    xs = step(nc_d, maps)

    xs = step(nc_m, [dict(moe_m, xin=xs[c]) for c in range(8)])

    return np.ascontiguousarray(
        np.stack([xs[c].T for c in range(8)], 0)).astype(np.float32)


# version 13 (final)
# speedup vs baseline: 1.0551x; 1.0551x over previous
import contextlib
import numpy as np
import ml_dtypes
import concourse.bass as bass
import concourse.mybir as mybir
from concourse.bass_utils import run_bass_kernel_spmd

F32 = mybir.dt.float32
BF16 = mybir.dt.bfloat16
AF = mybir.ActivationFunctionType
ALU = mybir.AluOpType
AX = mybir.AxisListType

T = 2048
D = 2048
NT = 16
TC = 512
NCH = T // TC

STREAMS = ("pe", "act", "dve", "pool", "sp")
CSEM = {"pe": "pe", "act": "act", "dve": "dve", "pool": "pool"}
SEMS = ("pe", "act", "dve", "pool", "spq0", "spq1", "poolq0", "poolq1")


def MM(out, lhsT, rhs, start, stop):
    return lambda e: e.matmul(out, lhsT, rhs, start=start, stop=stop,
                              skip_group_check=True)


def CP(out, in_):
    return lambda e: e.tensor_copy(out, in_)


def TT(out, a, b, op):
    return lambda e: e.tensor_tensor(out, a, b, op)


def TS1(out, in_, s, op):
    return lambda e: e.tensor_single_scalar(out, in_, s, op)


def TS2(out, in_, s1, s2, op0, op1):
    return lambda e: e.tensor_scalar(out, in_, s1, s2, op0, op1)


def STT(out, in0, s, in1, op0, op1):
    return lambda e: e.scalar_tensor_tensor(out, in0, s, in1, op0, op1)


def ACTV(out, in_, fn, bias=0.0, scale=1.0):
    return lambda e: e.activation(out, in_, fn, bias=bias, scale=scale)


def RMAX(out, in_, negate=False):
    return lambda e: e.reduce_max(out, in_, AX.X, negate=negate)


def RSUM(out, in_, absval=False):
    return lambda e: e.reduce_sum(out, in_, AX.X, apply_absolute_value=absval)


def RECIP(out, in_):
    return lambda e: e.reciprocal(out, in_)


def RSUMC(out, in_):
    return lambda e: e.reduce_sum(out, in_, AX.C)


def DMA(out, in_):
    return lambda e: e.dma_start(out, in_)


def MEMSET(ap, v):
    return lambda e: e.memset(ap, v)


class Plan:

    def __init__(self):
        self.ops = {s: [] for s in STREAMS}
        self.cnt = dict.fromkeys(SEMS, 0)
        self.ep = {"sp": 0, "pool": 0}
        self.ep_open = {"sp": False, "pool": False}
        self.ep_end = dict.fromkeys(
            [("sp", 0), ("sp", 1), ("pool", 0), ("pool", 1)], 0)
        self.bar = None

    def _push(self, stream, waits, fn, inc_sem, inc):
        own = CSEM.get(stream)
        if own in ("act", "dve", "pool"):
            waits = list(waits) + [(own, self.cnt[own])]
        if self.bar is not None:
            waits = list(waits) + self.bar
        self.ops[stream].append((list(waits), fn, inc_sem, inc))
        if inc_sem is not None:
            self.cnt[inc_sem] += inc
            return self.cnt[inc_sem]
        return None

    def pe(self, fn, waits=(), tick=False):
        return self._push("pe", waits, fn, "pe" if tick else None, 1)

    def act(self, fn, waits=(), tick=False):
        return self._push("act", waits, fn, "act" if tick else None, 1)

    def dve(self, fn, waits=(), tick=False):
        return self._push("dve", waits, fn, "dve" if tick else None, 1)

    def pool(self, fn, waits=(), tick=False):
        return self._push("pool", waits, fn, "pool" if tick else None, 1)

    def _qsem(self, stream):
        return ("spq%d" if stream == "sp" else "poolq%d") % (self.ep[stream] % 2)

    def dma(self, stream, fn, waits=()):
        qn = self._qsem(stream)
        waits = list(waits)
        if not self.ep_open[stream]:
            waits.append((qn, self.ep_end[(stream, self.ep[stream] % 2)]))
            self.ep_open[stream] = True
        return self._push(stream, waits, fn, qn, 16)

    def end_ep(self, stream):
        qn = self._qsem(stream)
        tokv = self.cnt[qn]
        self.ep_end[(stream, self.ep[stream] % 2)] = tokv
        self.ep[stream] += 1
        self.ep_open[stream] = False
        return (qn, tokv)

    def dma1(self, stream, fn, waits=()):
        self.dma(stream, fn, waits)
        return self.end_ep(stream)

    def set_barrier(self):
        self.bar = [(s, self.cnt[s]) for s in SEMS]

    def tok(self, sem):
        return self.cnt[sem]

    GO = 3_000_000

    def emit(self, nc):
        with contextlib.ExitStack() as st:
            sems = {s: st.enter_context(nc.semaphore("s_" + s)) for s in SEMS}
            go = st.enter_context(nc.semaphore("s_go"))
            block = st.enter_context(nc.Block())

            def run_stream(eng, stream):
                if stream == "pool":
                    eng.sem_clear(go)
                    for s in SEMS:
                        eng.sem_clear(sems[s])
                    eng.sem_inc(go, Plan.GO)
                else:
                    eng.wait_ge(go, Plan.GO)
                seen = dict.fromkeys(SEMS, 0)
                for waits, fn, inc_sem, inc in self.ops[stream]:
                    for wsem, wval in waits:
                        if wval <= 0:
                            continue
                        if wval > seen[wsem]:
                            eng.wait_ge(sems[wsem], wval)
                            seen[wsem] = wval
                    instr = fn(eng)
                    if inc_sem is not None:
                        instr.then_inc(sems[inc_sem], inc)

            @block.tensor
            def _(t):
                run_stream(t, "pe")

            @block.scalar
            def _(s):
                run_stream(s, "act")

            @block.vector
            def _(v):
                run_stream(v, "dve")

            @block.gpsimd
            def _(g):
                run_stream(g, "pool")

            @block.sync
            def _(sy):
                run_stream(sy, "sp")
                for sem in SEMS:
                    if self.cnt[sem] > 0:
                        sy.wait_ge(sems[sem], self.cnt[sem])
        with contextlib.ExitStack() as st2:
            block2 = st2.enter_context(nc.Block())

            @block2.gpsimd
            def _(g):
                g.sem_clear(go)
                for sem in SEMS:
                    g.sem_clear(sems[sem])
        return nc


AR_COLS = 78336


class Bufs:
    def __init__(self, nc, stack):
        ctx = stack.enter_context
        self.arena = ctx(nc.sbuf_tensor("arena", [128, AR_COLS], BF16))
        self.arena_f = self.arena.bitcast(F32)
        self.qk = ctx(nc.sbuf_tensor("qk", [128, 6144], BF16))
        self.ev = ctx(nc.sbuf_tensor("ev", [128, 2048], F32))
        self.evb = ctx(nc.sbuf_tensor("evb", [128, 1024], BF16))
        self.srA = ctx(nc.sbuf_tensor("srA", [1, 1536], F32))
        self.srB = ctx(nc.sbuf_tensor("srB", [1, 4096], BF16))
        self.colA = ctx(nc.sbuf_tensor("colA", [128, 16], F32))
        self.colB = ctx(nc.sbuf_tensor("colB", [128, 8], F32))
        self.ones_b = ctx(nc.sbuf_tensor("ones_b", [128, 512], BF16))
        self.ones_f = ctx(nc.sbuf_tensor("ones_f", [1, 128], F32))
        self.msk = ctx(nc.sbuf_tensor("msk", [128, 2048], BF16))
        self.gsc = ctx(nc.sbuf_tensor("gsc", [8, 2048], F32))
        self.sel = ctx(nc.sbuf_tensor("sel", [8, 1024], F32))
        self.ps = [ctx(nc.psum_tensor("ps%d" % i, [128, 512], F32))
                   for i in range(8)]

    def bf(self, off, cols):
        assert off % 2 == 0 and off // 2 + cols <= AR_COLS, (off, cols)
        return self.arena[:, off // 2: off // 2 + cols]

    def f32(self, off, cols):
        assert off % 4 == 0 and off // 4 + cols <= AR_COLS // 2, (off, cols)
        return self.arena_f[:, off // 4: off // 4 + cols]


Y_OFF = 0
A2 = 65536
A3 = 131072
A4 = 155648
AR_END = 156672
assert AR_END == AR_COLS * 2


def ysl(B, kd, c0, cn):
    return B.bf(Y_OFF + kd * 2 * T + 2 * c0, cn)


def setup_const(p, B):
    p.pool(MEMSET(B.ones_b[:, :], 1.0), tick=True)
    p.k_const = p.pool(MEMSET(B.ones_f[:, :], 1.0), tick=True)
    return p.k_const


def emit_ln(p, B, xin):
    xf = [B.f32(A2 + kd * 2048, TC) for kd in range(NT)]
    t1 = [B.f32(A2 + 32768 + kd * 2048, TC) for kd in range(NT)]
    xb = [B.bf(A3 + kd * 1024, TC) for kd in range(NT)]
    sq = xb
    ps_s, ps_r = B.ps[0], B.ps[1]
    mrow = B.srA[0:1, 0:512]
    vrow = B.srA[0:1, 512:1024]
    sd = B.srA[0:1, 1024:1536]

    prev_done = p.tok("dve")
    for tc in range(NCH):
        t0 = tc * TC
        for kd in range(NT):
            src = xin.ap()[kd * 128:(kd + 1) * 128, t0:t0 + TC]
            p.dma("sp", DMA(xf[kd], src), waits=[("dve", prev_done)])
        tok_f = p.end_ep("sp")
        for kd in range(NT):
            src = xin.ap()[kd * 128:(kd + 1) * 128, t0:t0 + TC]
            p.dma("pool", DMA(xb[kd], src),
                  waits=[("dve", prev_done), ("act", p.tok("act"))])
        tok_b = p.end_ep("pool")
        for kd in range(NT):
            p.pe(MM(ps_s[0:1, :], B.ones_b[:, 0:1], xb[kd], kd == 0, kd == NT - 1),
                 waits=[tok_b, ("dve", prev_done),
                        ("pool", getattr(p, "k_const", 0))],
                 tick=(kd == NT - 1))
        tk_sum = p.tok("pe")
        tk_m = p.dve(CP(mrow, ps_s[0:1, :]), waits=[("pe", tk_sum)], tick=True)
        tk_rep = p.pe(MM(ps_r[:, :], B.ones_f[0:1, :], mrow, True, True),
                      waits=[("dve", tk_m)], tick=True)
        for kd in range(NT):
            p.dve(STT(t1[kd], ps_r[:, :], -1.0 / D, xf[kd], ALU.mult, ALU.add),
                  waits=[("pe", tk_rep), tok_f], tick=(kd == NT - 1))
        tk_t1 = p.tok("dve")
        for kd in range(NT):
            p.act(ACTV(sq[kd], t1[kd], AF.Square),
                  waits=[("dve", tk_t1), ("pe", tk_sum)], tick=(kd == NT - 1))
        tk_sq = p.tok("act")
        for kd in range(NT):
            p.pe(MM(ps_s[0:1, :], B.ones_b[:, 0:1], sq[kd], kd == 0, kd == NT - 1),
                 waits=[("act", tk_sq), ("dve", tk_m)], tick=(kd == NT - 1))
        tk_vs = p.tok("pe")
        tk_v = p.dve(TS2(vrow, ps_s[0:1, :], 1.0 / D, 1e-5, ALU.mult, ALU.add),
                     waits=[("pe", tk_vs)], tick=True)
        tk_sd = p.act(ACTV(sd, vrow, AF.Sqrt), waits=[("dve", tk_v)], tick=True)
        tk_rr = p.dve(RECIP(vrow, sd), waits=[("act", tk_sd)], tick=True)
        tk_rep2 = p.pe(MM(ps_r[:, :], B.ones_f[0:1, :], vrow, True, True),
                       waits=[("dve", tk_rr)], tick=True)
        for kd in range(NT):
            p.dve(TT(ysl(B, kd, t0, TC), t1[kd], ps_r[:, :], ALU.mult),
                  waits=[("pe", tk_rep2)], tick=(kd == NT - 1))
        prev_done = p.tok("dve")
    return prev_done


def emit_wo_residual(p, B, io, has_bo, tok_ctx):
    xin, xout, wo = io["xin"], io["xout"], io["wo"]
    wocb = [B.bf(A3 + 16384 + i * 4096, 2048) for i in range(2)]
    wo_guard = [p.tok("pe"), p.tok("pe")]
    bo_sb = io.get("bo_sb")
    xold = [B.ev[:, 0:512], B.ev[:, 512:1024]]
    xold_guard = [p.tok("dve"), p.tok("dve")]
    xnew = [B.ev[:, 1024:1536], B.evb.bitcast(F32)[:, 0:512]]
    xnew_guard = [(None, 0), (None, 0)]
    it = 0
    for m in range(NT):
        par = m % 2
        src = wo.ap()[:, m * 128:(m + 1) * 128].rearrange("(k p) m -> p k m", p=128)
        ld = p.dma1("pool", DMA(wocb[par][:].rearrange("p (k m) -> p k m", m=128),
                                src), waits=[("pe", wo_guard[par])])
        for tc in range(NCH):
            psp = B.ps[4 + (it % 2)]
            for kd in range(NT):
                wts = [ld, tok_ctx]
                if kd == 0:
                    wts.append(("dve", p.tok("dve")))
                p.pe(MM(psp[:, :], wocb[par][:, kd * 128:(kd + 1) * 128],
                        B.bf(A2 + kd * 2 * T + 2 * tc * TC, TC),
                        kd == 0, kd == NT - 1),
                     waits=wts, tick=(kd == NT - 1))
            tk_mm = p.tok("pe")
            wo_guard[par] = tk_mm
            xop = it % 2
            ldx = p.dma1("sp", DMA(xold[xop], xin.ap()[m * 128:(m + 1) * 128,
                                                       tc * TC:(tc + 1) * TC]),
                         waits=[("dve", xold_guard[xop])])
            bias = bo_sb[:, m:m + 1] if has_bo else 0.0
            w2 = [("pe", tk_mm), ldx]
            if xnew_guard[xop][0] is not None:
                w2.append(xnew_guard[xop])
            tk_add = p.dve(STT(xnew[xop], psp[:, :], bias, xold[xop],
                               ALU.add, ALU.add), waits=w2, tick=True)
            xold_guard[xop] = tk_add
            xnew_guard[xop] = p.dma1(
                "sp", DMA(xout.ap()[m * 128:(m + 1) * 128, tc * TC:(tc + 1) * TC],
                          xnew[xop]), waits=[("dve", tk_add)])
            it += 1


def emit_attn(p, B, io, cfg):
    H, dh, KVH = cfg["H"], cfg["dh"], cfg["KVH"]
    causal = cfg["causal"]
    rope = cfg.get("rope")
    has_b = cfg.get("bias", False)
    scale = 1.0 / float(np.sqrt(dh))
    WV = KVH * dh
    grp = H // KVH
    h2 = dh // 2
    gsa = cfg.get("gsa", False)
    NSL = 4 if gsa else 8
    mha = (grp == 1)
    VG = 512 // dh if mha else 0

    xin, xout = io["xin"], io["xout"]
    wq, wk, wv, wo, ctxh = io["wq"], io["wk"], io["wv"], io["wo"], io["ctxh"]

    setup_const(p, B)
    tk_msk = (None, 0)
    if causal:
        tk_msk = p.dma1("sp", DMA(B.msk[:, :], io["mask4"].ap()))
    tk_y = emit_ln(p, B, xin)

    V2 = [B.bf(A2 + i * 16384, NT * 512) for i in range(2)]
    Qb = [B.bf(A2 + 32768 + i * 4096, 2048)[0:dh, :] for i in range(2)]
    Kb = [B.bf(A2 + 40960 + i * 4096, 2048)[0:dh, :] for i in range(2)]
    wqcb2 = [B.bf(A2 + 49152 + i * 4096, NT * dh) for i in range(2)]
    wkcb2 = [B.bf(A2 + 57344 + i * 4096, NT * dh) for i in range(2)]
    wvj = [B.bf(A3 + 16384 + i * 4096, NT * 128) for i in range(2)]
    Pt = [B.bf(A3 + i * 1024, 512) for i in range(NSL)]
    rp1 = B.qk[0:dh, 0:2048]
    rp2 = B.qk[0:dh, 2048:4096]
    sqb = B.qk[0:dh, 4096:4608]
    sqb2 = B.qk[0:dh, 5154:5666]
    qss = B.srB[0:1, 0:2048]
    kss = B.srB[0:1, 2048:4096]
    ps_st4 = [B.ps[2], B.ps[3], B.ps[5], B.ps[6]]
    ps_st = [B.ps[2], B.ps[3]]
    ps_sum, ps_rep, ps_ctx, ps_pr = B.ps[0], B.ps[1], B.ps[4], B.ps[7]

    qmaxс = [B.colB[0:1, 0:1], B.colB[0:1, 1:2]]
    kmaxс = [B.colB[0:1, 2:3], B.colB[0:1, 3:4]]
    nbcell = [B.colB[0:1, 4:5], B.colB[0:1, 5:6]]
    nbcol = [B.colB[:, 6:7], B.colB[:, 7:8]]

    ropeC = ropeS = None
    tk_rope = (None, 0)
    if rope is not None:
        ropeC = B.bf(A3 + 8192, 2048)[0:dh, :]
        ropeS = B.bf(A3 + 12288, 2048)[0:dh, :]
        p.dma("pool", DMA(ropeC, io["ropeC"].ap()[0:dh, :]),
              waits=[("dve", tk_y)])
        p.dma("pool", DMA(ropeS, io["ropeS"].ap()[0:dh, :]))
        tk_rope = p.end_ep("pool")

    bq = io.get("bq_sb")
    bk = io.get("bk_sb")
    bvb = B.qk[0:1, 4642:5154]

    v_guard = [("pe", 0), ("pe", 0)]
    qk_guard = [("pe", 0), ("pe", 0)]
    wcb_guard = [("pe", 0), ("pe", 0)]
    wvj_guard = [("pe", 0), ("pe", 0)]
    kb_guard = [("pe", 0), ("pe", 0)]
    prod_done = {}
    tk_ctx_dma = [(None, 0), (None, 0)]
    ib_v = 0

    def emit_v_block(g, vpar):
        nonlocal ib_v
        tk_bv = (None, 0)
        if has_b:
            tk_bv = p.dma1("pool", DMA(bvb, io["bv"].ap()[0:1,
                                       g * 512:(g + 1) * 512]),
                           waits=[("pe", p.tok("pe"))])
        for jh in range(4):
            wp = ib_v % 2
            ib_v += 1
            for kd in range(NT):
                dst = wvj[wp][:, kd * 128:(kd + 1) * 128]
                srcap = wv.ap()[kd * 128:(kd + 1) * 128,
                                g * 512 + jh * 128:g * 512 + jh * 128 + 128]
                p.dma("pool", DMA(dst, srcap),
                      waits=[wvj_guard[wp], ("dve", tk_y),
                             ("act", p.tok("act"))])
            tok_wv = p.end_ep("pool")
            for m in range(NT):
                psv = ps_pr
                for kd in range(NT):
                    wts = [tok_wv]
                    if kd == 0:
                        wts.append(("dve", p.tok("dve")))
                        wts.append(v_guard[vpar])
                    last = (kd == NT - 1) and not has_b
                    p.pe(MM(psv[0:128, 0:128], ysl(B, kd, m * 128, 128),
                            wvj[wp][:, kd * 128:(kd + 1) * 128], kd == 0, last),
                         waits=wts, tick=last)
                if has_b:
                    p.pe(MM(psv[0:128, 0:128], B.ones_b[0:1, 0:128],
                            bvb[0:1, jh * 128:jh * 128 + 128], False, True),
                         waits=[tk_bv], tick=True)
                tk_mm = p.tok("pe")
                wvj_guard[wp] = ("pe", tk_mm)
                p.dve(CP(V2[vpar][:, m * 512 + jh * 128:m * 512 + jh * 128 + 128],
                         psv[0:128, 0:128]),
                      waits=[("pe", tk_mm)], tick=True)
        return p.tok("dve")

    def project_one(dst, wcb, w_dram, hcol, bias_ap, ld, stat_row, maxcell):
        roped = rope is not None
        tgt = rp1 if roped else dst
        ev_toks = []
        for tc in range(NCH):
            for kd in range(NT):
                wts = [ld]
                if kd == 0:
                    wts.append(("dve", p.tok("dve")))
                    wts.append(("act", p.tok("act")))
                p.pe(MM(ps_pr[0:dh, :], wcb[:, kd * dh:(kd + 1) * dh],
                        ysl(B, kd, tc * TC, TC), kd == 0, kd == NT - 1),
                     waits=wts, tick=(kd == NT - 1))
            tk_mm = p.tok("pe")
            b = bias_ap if bias_ap is not None else 0.0
            ev_toks.append(p.dve(TS1(tgt[:, tc * TC:(tc + 1) * TC],
                                     ps_pr[0:dh, :], b, ALU.add),
                                 waits=[("pe", tk_mm),
                                        io.get("bias_tok", (None, 0))],
                                 tick=True))
        tk_ev = ev_toks[-1]
        if roped:
            p.dve(CP(rp2[0:h2, :], rp1[h2:dh, :]),
                  waits=[("dve", tk_ev), tk_rope], tick=True)
            p.dve(CP(rp2[h2:dh, :], rp1[0:h2, :]), tick=True)
            for c4 in range(NCH):
                cs4 = slice(c4 * TC, (c4 + 1) * TC)
                p.dve(TT(sqb, rp2[:, cs4], ropeS[:, cs4], ALU.mult), tick=True)
                p.dve(TT(sqb2, rp1[:, cs4], ropeC[:, cs4], ALU.mult), tick=True)
                tk_ev = p.dve(TT(dst[:, cs4], sqb, sqb2, ALU.add), tick=True)
        red_toks = []
        for tc in range(NCH):
            tsq = p.dve(TT(sqb, dst[:, tc * TC:(tc + 1) * TC],
                           dst[:, tc * TC:(tc + 1) * TC], ALU.mult),
                        waits=[("dve", tk_ev), ("pe", p.tok("pe"))], tick=True)
            p.pe(MM(ps_pr[0:1, 0:512], B.ones_b[0:dh, 0:1], sqb,
                    True, True),
                 waits=[("dve", tsq), ("pool", p.k_const)], tick=True)
            red_toks.append(p.dve(CP(stat_row[0:1, tc * TC:(tc + 1) * TC],
                                     ps_pr[0:1, 0:512]),
                                  waits=[("pe", p.tok("pe"))], tick=True))
        tk_mx = p.dve(RMAX(maxcell, stat_row), waits=[("dve", red_toks[-1])],
                      tick=True)
        return tk_ev, tk_mx

    def produce(h):
        par = h % 2
        g = h // grp
        new_k = (h % grp == 0)
        srcq = wq.ap()[:, h * dh:(h + 1) * dh].rearrange("(k p) m -> p k m",
                                                         p=128)
        ldq = p.dma1("pool", DMA(
            wqcb2[par][:].rearrange("p (k m) -> p k m", m=dh), srcq),
            waits=[wcb_guard[par], ("dve", tk_y)])
        ldk = (None, 0)
        if new_k:
            srck = wk.ap()[:, g * dh:(g + 1) * dh].rearrange(
                "(k p) m -> p k m", p=128)
            ldk = p.dma1("pool", DMA(
                wkcb2[par][:].rearrange("p (k m) -> p k m", m=dh), srck),
                waits=[wcb_guard[par]])
        toks = {}
        if new_k:
            kpar = g % 2
            tk_kev, tk_kmx = project_one(
                Kb[kpar], wkcb2[par], wk, g,
                bk[0:dh, g:g + 1] if has_b else None, ldk, kss, kmaxс[kpar])
            toks["kev"], toks["kmx"] = tk_kev, tk_kmx
        tk_qev, tk_qmx = project_one(
            Qb[par], wqcb2[par], wq, h,
            bq[0:dh, h:h + 1] if has_b else None, ldq, qss, qmaxс[par])
        wcb_guard[par] = ("pe", p.tok("pe"))
        kpar = (h // grp) % 2
        kmx_tok = toks["kmx"] if new_k else prod_done[("kmx", h // grp)]
        tk_nb = p.pool(TS2(nbcell[par], qmaxс[par], kmaxс[kpar], -0.5 * scale,
                           ALU.add, ALU.mult),
                       waits=[("dve", tk_qmx), ("dve", kmx_tok)], tick=True)
        tk_nbr = p.pe(MM(ps_rep[:, 0:1], B.ones_f[0:1, :], nbcell[par],
                         True, True),
                      waits=[("pool", tk_nb), ("dve", p.tok("dve"))], tick=True)
        tk_nbc = p.dve(CP(nbcol[par], ps_rep[:, 0:1]), waits=[("pe", tk_nbr)],
                       tick=True)
        toks["qev"], toks["nbc"] = tk_qev, tk_nbc
        if new_k:
            prod_done[("kmx", h // grp)] = toks["kmx"]
        prod_done[h] = toks
        return toks

    if gsa:
        qa_sb = B.bf(A3 + 8192, 2048)[0:8, :]
        ka_sb = B.bf(A3 + 12288, 2048)[0:8, :]
        g0 = B.gsc[0:8, 0:512]
        g1 = B.gsc[0:8, 512:1024]
        g2 = B.gsc[0:8, 1024:1536]
        pos_nat = B.gsc[0:8, 1536:1664]
        dir_nat = B.gsc[0:8, 1664:1792]
        gtmp = B.gsc[0:8, 1792:1920]
        pn_nat = B.gsc[0:8, 1920:2048]
        pos_fm = B.ev[:, 1536:1544]
        dir_fm = B.ev[:, 1544:1552]
        posn_b = B.qk[:, 4608:4616]
        dirn_b = B.qk[:, 4616:4624]
        ampSka_b = B.qk[0:8, 4624:4625]
        possq_rb = B.qk[0:1, 4626:4634]
        negoff_rb = B.qk[0:1, 4634:4642]
        sc_c4 = B.colA[0:8, 4:5]
        sc_c5 = B.colA[0:8, 5:6]
        sc_c6 = B.colA[0:8, 6:7]
        possq_c = B.colA[0:8, 7:8]
        negoff_c = B.colA[0:8, 8:9]
        i2s2_c = B.colA[0:8, 9:10]
        ni2s2_c = B.colA[0:8, 10:11]
        amp_c = B.colA[0:8, 11:12]
        ds_c = B.colA[0:8, 12:13]
        row8a = B.srA[0:1, 768:776]
        row8b = B.srA[0:1, 776:784]
        obrow = B.srA[0:1, 512:640]
        blrow = B.srA[0:1, 640:768]
        winv_row = B.srA[0:1, 1024:1536]
        C0 = float(np.sqrt(128.0) * 0.3)

        def cell_chain(ops, first_waits=()):
            last = None
            for stream, fn in ops:
                w = [last] if last else list(first_waits)
                tk = getattr(p, stream)(fn, waits=w, tick=True)
                last = (stream, tk)
            return last

        def row_to_col(row_f32_1x8, col_dst, producer_tok, scale=1.0):
            tk_t = p.pe(MM(B.ps[1][0:8, 0:1], row_f32_1x8, B.ones_f[0:1, 0:1],
                           True, True),
                        waits=[producer_tok, ("dve", p.tok("dve")),
                               ("act", p.tok("act"))], tick=True)
            tk_e = p.act((lambda o=col_dst, s=scale: lambda e: e.mul(
                o, B.ps[1][0:8, 0:1], s))(), waits=[("pe", tk_t)], tick=True)
            return ("act", tk_e)

        def col_to_row(col_f32_8x1, row_dst, producer_tok, scale=1.0):
            tk_t = p.pe(MM(B.ps[1][0:1, 0:8], col_f32_8x1, id8_sb, True, True),
                        waits=[producer_tok, ("dve", p.tok("dve")),
                               ("act", p.tok("act"))], tick=True)
            tk_e = p.act((lambda o=row_dst, s=scale: lambda e: e.mul(
                o, B.ps[1][0:1, 0:8], s))(), waits=[("pe", tk_t)], tick=True)
            return ("act", tk_e)

        id8_sb = B.msk.bitcast(F32)[0:8, 0:8]
        tk_id = p.dma1("sp", DMA(id8_sb, io["id8"].ap()))
        tk_sc = p.dma1("sp", DMA(B.colA[0:1, 14:15], io["dstr"].ap()))
        tk_sc2 = p.dma1("sp", DMA(B.colA[0:1, 15:16], io["gstr"].ap()))
        tk_ds = p.act(ACTV(B.colA[0:1, 13:14], B.colA[0:1, 14:15], AF.Sigmoid),
                      waits=[tk_sc], tick=True)
        tk_gs = p.act(ACTV(B.colA[0:1, 14:15], B.colA[0:1, 15:16], AF.Sigmoid),
                      waits=[tk_sc2, ("act", tk_ds)], tick=True)
        tk_bl = p.dve(TS2(B.colA[0:1, 15:16], B.colA[0:1, 14:15], 0.1, 0.05,
                          ALU.mult, ALU.min), waits=[("act", tk_gs)], tick=True)
        tk_blr = p.dve(TS1(blrow, B.ones_f[0:1, :], B.colA[0:1, 15:16],
                           ALU.mult), waits=[("dve", tk_bl),
                                             ("pool", p.k_const)], tick=True)
        tk_ob = p.dve(TS2(B.colA[0:1, 14:15], B.colA[0:1, 15:16], -1.0, 1.0,
                          ALU.mult, ALU.add), waits=[("dve", tk_bl)], tick=True)
        tk_obr = p.dve(TS1(obrow, B.ones_f[0:1, :], B.colA[0:1, 14:15],
                           ALU.mult), waits=[("dve", tk_ob)], tick=True)
        tk_dsr = p.pe(MM(B.ps[1][0:8, 0:1], B.ones_f[0:1, 0:8],
                         B.colA[0:1, 13:14], True, True),
                      waits=[("act", tk_ds), ("dve", p.tok("dve"))], tick=True)
        tk_dsc = p.act((lambda: lambda e: e.copy(ds_c, B.ps[1][0:8, 0:1]))(),
                       waits=[("pe", tk_dsr)], tick=True)

        def gsa_prep_head(h, tk_headguard):
            hg = [tk_headguard, ("dve", p.tok("dve")), ("act", p.tok("act"))]
            e1 = p.dma1("sp", DMA(pos_nat, io["spos"].ap()[h, :, :]), waits=hg)
            e2 = p.dma1("sp", DMA(dir_nat, io["sdir"].ap()[h, :, :]), waits=hg)
            e3 = p.dma1("sp", DMA(pos_fm, io["sposT"].ap()[h, :, :]), waits=hg)
            e4 = p.dma1("sp", DMA(dir_fm, io["sdirT"].ap()[h, :, :]), waits=hg)
            e5 = p.dma1("sp", DMA(row8a, io["lsc"].ap()[h:h + 1, :]), waits=hg)
            e6 = p.dma1("sp", DMA(row8b, io["lam"].ap()[h:h + 1, :]), waits=hg)
            tk = cell_chain(first_waits=[e1], ops=[
                ("act", ACTV(gtmp, pos_nat, AF.Square)),
                ("dve", RSUM(sc_c4, gtmp)),
                ("act", ACTV(sc_c5, sc_c4, AF.Sqrt)),
                ("dve", TS1(sc_c4, sc_c5, 1e-12, ALU.add)),
                ("dve", RECIP(sc_c6, sc_c4)),
                ("dve", TT(sc_c4, sc_c5, sc_c6, ALU.mult)),
                ("dve", TS1(sc_c5, sc_c4, C0, ALU.mult)),
                ("dve", TT(possq_c, sc_c5, sc_c5, ALU.mult)),
                ("dve", TS1(sc_c5, sc_c6, C0, ALU.mult)),
                ("dve", TS1(pn_nat, pos_nat, sc_c5, ALU.mult)),
            ])
            tk_pn = tk
            tk_r2 = p.dve(TS1(sc_c4, sc_c5, -2.0, ALU.mult), waits=[tk_pn],
                          tick=True)
            r_row = col_to_row(sc_c4, B.srA[0:1, 784:792], ("dve", tk_r2))
            tk_rep = p.pe(MM(B.ps[1][:, 0:8], B.ones_f[0:1, :],
                             B.srA[0:1, 784:792], True, True),
                          waits=[r_row, ("dve", p.tok("dve"))], tick=True)
            tk_pb = p.dve(TT(posn_b, pos_fm, B.ps[1][:, 0:8], ALU.mult),
                          waits=[("pe", tk_rep), e3], tick=True)
            tk_d = cell_chain(first_waits=[e2, ("dve", p.tok("dve"))], ops=[
                ("act", ACTV(gtmp, dir_nat, AF.Square)),
                ("dve", RSUM(sc_c4, gtmp)),
                ("act", ACTV(sc_c5, sc_c4, AF.Sqrt)),
                ("dve", TS1(sc_c4, sc_c5, 1e-12, ALU.add)),
                ("dve", RECIP(sc_c5, sc_c4)),
                ("dve", TS1(gtmp, dir_nat, sc_c5, ALU.mult)),
            ])
            rd_row = col_to_row(sc_c5, B.srA[0:1, 792:800], tk_d)
            tk_rep2 = p.pe(MM(B.ps[1][:, 0:8], B.ones_f[0:1, :],
                              B.srA[0:1, 792:800], True, True),
                           waits=[rd_row, ("dve", tk_pb)], tick=True)
            tk_db = p.dve(TT(dirn_b, dir_fm, B.ps[1][:, 0:8], ALU.mult),
                          waits=[("pe", tk_rep2), e4], tick=True)
            tk_off = cell_chain([
                ("dve", TT(g0[:, 0:128], pn_nat, gtmp, ALU.mult)),
                ("dve", RSUM(sc_c4, g0[:, 0:128])),
                ("dve", TS1(negoff_c, sc_c4, -1.0, ALU.mult)),
            ])
            tk_s1 = p.act(ACTV(B.srA[0:1, 784:792], row8a, AF.Exp),
                          waits=[e5, ("pe", p.tok("pe")),
                                 ("dve", tk_off[1] if False else p.tok("dve"))],
                          tick=True)
            tk_s2 = p.dve(TS2(B.srA[0:1, 792:800], B.srA[0:1, 784:792],
                              1.2, 0.3, ALU.min, ALU.max),
                          waits=[("act", tk_s1)], tick=True)
            tk_s3 = p.pool(TT(B.srA[0:1, 784:792], B.srA[0:1, 792:800],
                              B.srA[0:1, 792:800], ALU.mult),
                           waits=[("dve", tk_s2)], tick=True)
            tk_s4 = p.dve(RECIP(B.srA[0:1, 792:800], B.srA[0:1, 784:792]),
                          waits=[("pool", tk_s3)], tick=True)
            tk_s5 = p.dve(TS1(B.srA[0:1, 784:792], B.srA[0:1, 792:800], 0.5,
                              ALU.mult), waits=[("dve", tk_s4)], tick=True)
            t_i2 = row_to_col(B.srA[0:1, 784:792], i2s2_c, ("dve", tk_s5))
            t_ni2 = cell_chain([
                ("dve", TS1(ni2s2_c, i2s2_c, -1.0, ALU.mult))],
                first_waits=[t_i2])
            tk_a1 = p.dve(RMAX(B.colA[0:1, 14:15], row8b, negate=True),
                          waits=[e6, t_i2], tick=True)
            tk_a2 = p.act(ACTV(B.srA[0:1, 784:792], row8b, AF.Exp,
                               bias=B.colA[0:1, 14:15]),
                          waits=[("dve", tk_a1)], tick=True)
            tk_a3 = p.dve(RSUM(B.colA[0:1, 14:15], B.srA[0:1, 784:792]),
                          waits=[("act", tk_a2)], tick=True)
            tk_a4 = p.dve(RECIP(B.colA[0:1, 15:16], B.colA[0:1, 14:15]),
                          waits=[("dve", tk_a3)], tick=True)
            tk_a5 = p.dve(TS1(B.srA[0:1, 792:800], B.srA[0:1, 784:792],
                              B.colA[0:1, 15:16], ALU.mult),
                          waits=[("dve", tk_a4)], tick=True)
            t_amp = row_to_col(B.srA[0:1, 792:800], amp_c, ("dve", tk_a5))
            tk_pr = p.act((lambda: lambda e: e.copy(
                possq_rb, B.ps[1][0:1, 0:8]))(), waits=[
                col_to_row(possq_c, B.srA[0:1, 784:792], tk_off),
            ], tick=True) if False else None
            r1 = col_to_row(possq_c, B.srA[0:1, 784:792], t_amp)
            tk_pq = p.dve(CP(possq_rb, B.srA[0:1, 784:792]), waits=[r1],
                          tick=True)
            r2 = col_to_row(negoff_c, B.srA[0:1, 792:800], ("dve", tk_pq))
            tk_no = p.dve(CP(negoff_rb, B.srA[0:1, 792:800]), waits=[r2],
                          tick=True)
            return {"pb": ("dve", tk_pb), "db": ("dve", tk_db),
                    "rows": ("dve", tk_no), "ds": ("act", tk_dsc)}

        def gsa_affinity(out_sb, x_ap, xsq_row, prep, fold_amp):
            sub = cfg.get("gsa_sub", "C")
            last = (None, 0)
            for c in range(NCH):
                cs = slice(c * TC, (c + 1) * TC)
                p.pe(MM(B.ps[2][0:8, :], posn_b, x_ap[:, cs], True,
                        sub == "A0"),
                     waits=[prep["pb"], prep["rows"],
                            ("dve", p.tok("dve")), ("act", p.tok("act"))],
                     tick=(sub == "A0"))
                if sub != "A0":
                    p.pe(MM(B.ps[2][0:8, :], B.ones_b[0:1, 0:8],
                            xsq_row[0:1, cs], False, False), waits=[])
                    p.pe(MM(B.ps[2][0:8, :], possq_rb, B.ones_b[0:1, 0:512],
                            False, True), waits=[], tick=True)
                tk_d2 = p.tok("pe")
                if sub in ("A0", "A"):
                    tx = p.dve(TS2(out_sb[:, cs], B.ps[2][0:8, :], 1e-3, 0.0,
                                   ALU.mult, ALU.max),
                               waits=[("pe", tk_d2), ("dve", p.tok("dve"))],
                               tick=True)
                    last = ("dve", tx)
                    continue
                p.pe(MM(B.ps[3][0:8, :], dirn_b, x_ap[:, cs], True, False),
                     waits=[prep["db"]])
                p.pe(MM(B.ps[3][0:8, :], negoff_rb, B.ones_b[0:1, 0:512],
                        False, True), waits=[], tick=True)
                tk_pj = p.tok("pe")
                ta = p.dve(TS2(g0, B.ps[2][0:8, :], ni2s2_c, 0.0,
                               ALU.mult, ALU.min),
                           waits=[("pe", tk_d2), ("dve", p.tok("dve"))],
                           tick=True)
                tb = p.act(ACTV(g1, B.ps[3][0:8, :], AF.Square),
                           waits=[("pe", tk_pj), ("act", p.tok("act"))],
                           tick=True)
                tc_ = p.dve(STT(g2, g1, i2s2_c, g0, ALU.mult, ALU.add),
                            waits=[("act", tb), ("dve", ta)], tick=True)
                td = p.dve(TS1(g1, g2, 0.0, ALU.min), waits=[("dve", tc_)],
                           tick=True)
                te = p.act(ACTV(g2, g0, AF.Exp), waits=[("dve", td)], tick=True)
                tf = p.act(ACTV(g0, g1, AF.Exp), waits=[("act", te)], tick=True)
                tg = p.dve(TT(g1, g0, g2, ALU.subtract), waits=[("act", tf)],
                           tick=True)
                if fold_amp:
                    th = p.dve(STT(g0, g1, ds_c, g2, ALU.mult, ALU.add),
                               waits=[("dve", tg), prep["ds"]], tick=True)
                    ti = p.dve(TS1(out_sb[:, cs], g0, amp_c, ALU.mult),
                               waits=[("dve", th)], tick=True)
                    last = ("dve", ti)
                else:
                    th = p.dve(STT(out_sb[:, cs], g1, ds_c, g2,
                                   ALU.mult, ALU.add),
                               waits=[("dve", tg), prep["ds"]], tick=True)
                    last = ("dve", th)
            return last


    if mha:
        emit_v_block(0, 0)
    else:
        for gg in range(1):
            pass
        emit_v_block(0, 0)
    produce(0)

    for h in range(H):
        par = h % 2
        g = h // grp
        kpar = g % 2
        vpar = (h // VG) % 2 if mha else 0
        vbase = (h % VG) * dh if mha else g * dh
        Q_ap = Qb[par]
        K_ap = Kb[kpar]
        toks = prod_done[h]
        tk_qstat = toks["qev"]
        tk_nbc = toks["nbc"]

        if gsa:
            prep = gsa_prep_head(h, ("pe", p.tok("pe")))
            tk_qa = gsa_affinity(qa_sb, Q_ap, qss, prep, False)
            tk_ka = gsa_affinity(ka_sb, K_ap, kss, prep, True)
            tk_ska = p.dve(RSUM(B.colA[0:8, 14:15], ka_sb),
                           waits=[tk_ka], tick=True)
            tk_skb = p.dve(CP(ampSka_b, B.colA[0:8, 14:15]),
                           waits=[("dve", tk_ska)], tick=True)

        bank_exp = [0, 0, 0, 0]
        slot_free = [("pe", 0)] * NSL
        LAG = 3

        def attn_tc(tc):
            nonlocal tk_ctx_dma
            ktmax = 4 * tc + 4 if causal else NT
            exp_tok = {}

            def emit_sums_av(j, last):
                sl = j % NSL
                wsm = [("act", exp_tok[j])]
                if j == 0:
                    wsm.append(("dve", p.tok("dve")))
                p.pe(MM(ps_sum[0:1, :], B.ones_b[:, 0:1], Pt[sl],
                        j == 0, last), waits=wsm, tick=last)
                p.pe(MM(ps_ctx[0:dh, :],
                        V2[vpar][:, j * 512 + vbase: j * 512 + vbase + dh],
                        Pt[sl], j == 0, last),
                     waits=[("act", exp_tok[j])] +
                           ([("dve", p.tok("dve")), v_guard[vpar] if False
                             else (None, 0)] if j == 0 else []),
                     tick=True)
                slot_free[sl] = ("pe", p.tok("pe"))

            for j in range(ktmax):
                sl = j % NSL
                bnk = j % 4
                psj = ps_st4[bnk]
                st_w = [("act", bank_exp[bnk]), ("dve", tk_qstat)]
                if bank_exp[bnk] == 0:
                    st_w.append(("dve", p.tok("dve")))
                p.pe(MM(psj[:, :], K_ap[:, j * 128:(j + 1) * 128],
                        Q_ap[:, tc * TC:(tc + 1) * TC], True, True),
                     waits=st_w, tick=True)
                dep = ("pe", p.tok("pe"))
                if causal and j >= 4 * tc:
                    u = j - 4 * tc
                    tk_mk = p.dve(TT(psj[:, :], psj[:, :],
                                     B.msk[:, u * 512:(u + 1) * 512],
                                     ALU.add),
                                  waits=[dep, tk_msk], tick=True)
                    dep = ("dve", tk_mk)
                bank_exp[bnk] = p.act(
                    ACTV(Pt[sl], psj[:, :], AF.Exp,
                         bias=nbcol[par], scale=scale),
                    waits=[dep, ("dve", tk_nbc), slot_free[sl]], tick=True)
                exp_tok[j] = bank_exp[bnk]
                if j >= LAG:
                    emit_sums_av(j - LAG, last=False)
            for j in range(max(0, ktmax - LAG), ktmax):
                emit_sums_av(j, last=(j == ktmax - 1))
            tk_avsum = p.tok("pe")
            inv_row = B.srA[0:1, 0:512]
            tk_inv = p.dve(RECIP(inv_row, ps_sum[0:1, :]),
                           waits=[("pe", tk_avsum)], tick=True)
            cpar = tc % 2
            ctx_st = B.evb[0:dh, cpar * 512:cpar * 512 + 512]
            if not gsa:
                tk_rp = p.pe(MM(ps_rep[:, :], B.ones_f[0:1, :], inv_row,
                                True, True),
                             waits=[("dve", tk_inv)], tick=True)
                tk_re = p.dve(CP(B.ev[:, 0:512], ps_rep[:, :]),
                              waits=[("pe", tk_rp)], tick=True)
                tk_cx = p.dve(TT(ctx_st, ps_ctx[0:dh, :], B.ev[0:dh, 0:512],
                                 ALU.mult),
                              waits=[("dve", tk_re), tk_ctx_dma[cpar]],
                              tick=True)
            else:
                tk_wsum = p.pe(MM(ps_sum[0:1, :], ampSka_b,
                                  qa_sb[:, tc * TC:(tc + 1) * TC],
                                  True, True),
                               waits=[("dve", tk_inv), ("dve", tk_skb)],
                               tick=True)
                bank_w = [("act", 0), ("act", 0)]
                Wt = [B.bf(A3 + 4096 + i * 1024, 512) for i in range(NSL)]
                for b0 in range(0, NT, NSL):
                    for j in range(b0, b0 + NSL):
                        sl = j - b0
                        bnk = j % 2
                        psj = ps_st[bnk]
                        p.pe(MM(psj[:, :], ka_sb[:, j * 128:(j + 1) * 128],
                                qa_sb[:, tc * TC:(tc + 1) * TC],
                                True, True),
                             waits=[bank_w[bnk]], tick=True)
                        bank_w[bnk] = ("act", p.act(
                            (lambda o=Wt[sl], i_=psj: lambda e: e.copy(
                                o, i_[:, :]))(),
                            waits=[("pe", p.tok("pe")),
                                   ("pe", p.tok("pe"))], tick=True))
                    tk_wexp = p.tok("act")
                    for j in range(b0, b0 + NSL):
                        p.pe(MM(B.ps[7][0:dh, :],
                                V2[vpar][:, j * 512 + vbase:
                                         j * 512 + vbase + dh],
                                Wt[j - b0], j == 0, j == NT - 1),
                             waits=[("act", tk_wexp),
                                    ("dve", p.tok("dve"))],
                             tick=(j == NT - 1))
                tk_gav = p.tok("pe")
                tk_w1 = p.dve(TS1(B.gsc[0:1, 0:512], ps_sum[0:1, :],
                                  1e-8, ALU.add),
                              waits=[("pe", tk_gav),
                                     ("dve", p.tok("dve"))], tick=True)
                tk_w2 = p.dve(RECIP(winv_row[0:1, 0:512],
                                    B.gsc[0:1, 0:512]),
                              waits=[("dve", tk_w1)], tick=True)
                tk_rp1 = p.pe(MM(ps_rep[:, :], obrow, inv_row, True, True),
                              waits=[("dve", tk_obr), ("dve", tk_w2),
                                     ("dve", p.tok("dve"))], tick=True)
                tk_re1 = p.dve(CP(B.ev[:, 0:512], ps_rep[:, :]),
                               waits=[("pe", tk_rp1)], tick=True)
                tk_t1 = p.dve(TT(B.ev[:, 512:1024], ps_ctx[:, :],
                                 B.ev[:, 0:512], ALU.mult),
                              waits=[("dve", tk_re1)], tick=True)
                tk_rp2 = p.pe(MM(ps_rep[:, :], blrow,
                                 winv_row[0:1, 0:512], True, True),
                              waits=[("dve", tk_blr), ("dve", tk_t1)],
                              tick=True)
                tk_re2 = p.dve(CP(B.ev[:, 0:512], ps_rep[:, :]),
                               waits=[("pe", tk_rp2)], tick=True)
                tk_t2 = p.dve(TT(B.ev[:, 1536:2048], B.ps[7][:, :],
                                 B.ev[:, 0:512], ALU.mult),
                              waits=[("dve", tk_re2)], tick=True)
                tk_cx = p.dve(TT(ctx_st, B.ev[:, 512:1024],
                                 B.ev[:, 1536:2048], ALU.add),
                              waits=[("dve", tk_t2), tk_ctx_dma[cpar]],
                              tick=True)

            tk_ctx_dma[cpar] = p.dma1("sp", DMA(
                ctxh.ap()[h * dh:(h + 1) * dh, tc * TC:(tc + 1) * TC],
                ctx_st), waits=[("dve", tk_cx)])

        attn_tc(0)
        attn_tc(1)
        if h + 1 < H:
            if mha and (h + 1) % VG == 0:
                nvpar = ((h + 1) // VG) % 2
                emit_v_block((h + 1) // VG, nvpar)
            produce(h + 1)
        attn_tc(2)
        attn_tc(3)
        v_guard[vpar] = ("pe", p.tok("pe"))
        qk_guard[par] = ("pe", p.tok("pe"))

    for kd in range(NT):
        p.dma("sp", DMA(B.bf(A2 + kd * 2 * T, T),
                        ctxh.ap()[kd * 128:(kd + 1) * 128, :]),
              waits=[("pe", p.tok("pe")), ("dve", p.tok("dve")),
                     tk_ctx_dma[0], tk_ctx_dma[1]])
    tok_ctx = p.end_ep("sp")
    emit_wo_residual(p, B, io, io.get("bo_sb") is not None, tok_ctx)


def emit_ffn0(p, B, io, do_quant=True):
    xin, xout = io["xin"], io["xout"]
    w1t, w2t = io["w1t"], io["w2t"]
    w1q, w2q = io["w1q"], io["w2q"]
    b1_sb, b2_sb = io["b1_sb"], io["b2_sb"]

    setup_const(p, B)
    tk_y = emit_ln(p, B, xin)

    qsl = [B.bf(A2 + i * 16384, 8192) for i in range(2)]
    fsl = [B.f32(A2 + 32768 + i * 8192, 2048) for i in range(2)]
    absw = B.f32(A2 + 49152, 2048)
    gew = B.f32(A2 + 57344, 2048)
    sgn = [B.bf(A3, 2048), B.bf(A3 + 12288, 2048)]
    qot = [B.bf(A3 + 4096 + i * 4096, 2048) for i in range(2)]
    redbuf = B.ev[:, 1024:1152]
    acc_col = B.colA[:, 4:5]
    acc_hbm = io["acc_hbm"]
    asum_row = B.srA[0:1, 0:128]
    cell = B.colA[0:1, 9:10]
    halfcell = B.colA[0:1, 10:11]
    shalf = B.colA[:, 11:12]
    store_toks = []
    wlist = (((w1t, w1q, 2048, 8192), (w2t, w2q, 8192, 2048))
             if do_quant else ())
    sl_guard = [("dve", tk_y), ("dve", tk_y)]
    f_guard = [("dve", 0), ("dve", 0)]
    q_guard = [(None, 0), (None, 0)]
    sgn_guard = [0, 0]
    absw_guard = ("dve", 0)
    for (wt, wq_s, nrow, ncol) in wlist:
        nel = nrow * ncol
        i = 0
        cw = min(8192, ncol)
        red_war = ("pool", p.tok("pool"))
        for r0 in range(0, nrow, 128):
            for c0 in range(0, ncol, cw):
                par = i % 2
                ld = p.dma1("pool", DMA(qsl[par][:, 0:cw],
                                        wt.ap()[r0:r0 + 128, c0:c0 + cw]),
                            waits=[sl_guard[par], ("act", p.tok("act"))])
                tk_r = p.dve(RSUM(redbuf[:, i:i + 1], qsl[par][:, 0:cw],
                                  absval=True),
                             waits=[ld, red_war], tick=True)
                sl_guard[par] = ("dve", tk_r)
                i += 1
        nslab = i
        tk_acc = p.dve(RSUM(acc_col, redbuf[:, 0:nslab]),
                       waits=[("dve", p.tok("dve"))], tick=True)
        tk_as = p.pool(RSUMC(B.srA[0:1, 128:129], acc_col),
                       waits=[("dve", tk_acc)], tick=True)
        tk_c = p.dve(TS2(cell, B.srA[0:1, 128:129], 1.0 / nel, 1e-8,
                         ALU.mult, ALU.max),
                     waits=[("pool", tk_as)], tick=True)
        tk_h = p.act((lambda: lambda e: e.mul(halfcell, cell, 0.5))(),
                     waits=[("dve", tk_c)], tick=True)
        tk_rp = p.pe(MM(B.ps[1][:, 0:1], B.ones_f[0:1, :], halfcell,
                        True, True),
                     waits=[("act", tk_h), ("dve", p.tok("dve"))], tick=True)
        tk_sh = p.act((lambda: lambda e: e.copy(shalf, B.ps[1][:, 0:1]))(),
                      waits=[("pe", tk_rp)], tick=True)
        red_war = ("pool", tk_c)
        i = 0
        last_store = (None, 0)
        for r0 in range(0, nrow, 128):
            for c0 in range(0, ncol, 2048):
                par = i % 2
                ld = p.dma1("pool", DMA(fsl[par], wt.ap()[r0:r0 + 128,
                                                          c0:c0 + 2048]),
                            waits=[f_guard[par]])
                ts = p.act(ACTV(sgn[par], fsl[par], AF.Sign),
                           waits=[ld, ("dve", sgn_guard[par])], tick=True)
                ta = p.act(ACTV(absw, fsl[par], AF.Abs),
                           waits=[ld, absw_guard], tick=True)
                f_guard[par] = ("act", ta)
                tg = p.dve(TS1(gew, absw, shalf, ALU.is_ge),
                           waits=[("act", tk_sh), ("act", ta)], tick=True)
                absw_guard = ("dve", tg)
                tq = p.dve(TT(qot[par], gew, sgn[par], ALU.mult),
                           waits=[("act", ts), q_guard[par]], tick=True)
                sgn_guard[par] = tq
                last_store = p.dma1("sp", DMA(
                    wq_s.ap()[r0:r0 + 128, c0:c0 + 2048], qot[par]),
                    waits=[("dve", tq)])
                q_guard[par] = last_store
                i += 1
        store_toks.append(last_store)
    if do_quant:
        tok_w1_done, tok_w2_done = store_toks
    else:
        tok_w1_done = tok_w2_done = (None, 0)

    h_guard = ("dve", p.tok("dve"))
    cb = [B.bf(A3 + i * 8192, 16 * 256) for i in range(3)]
    cb_guard = [("pe", p.tok("pe"))] * 3
    xold = [B.ev[:, 0:512], B.ev[:, 512:1024]]
    xold_guard = [p.tok("dve"), p.tok("dve")]
    xnew = [B.ev[:, 1024:1536], B.evb.bitcast(F32)[:, 0:512]]
    xnew_guard = [(None, 0), (None, 0)]
    h_evac_last = ("act", p.tok("act"))
    it = 0
    ib = 0
    for tc in range(NCH):
        for fg in range(32):
            par = ib % 3
            ib += 1
            ld = p.dma1("pool", DMA(
                cb[par][:].rearrange("p (k m) -> p k m", m=256),
                io["w1q"].ap()[:, fg * 256:(fg + 1) * 256].rearrange(
                    "(k p) m -> p k m", p=128)),
                waits=[cb_guard[par], tok_w1_done, tok_w2_done, h_guard])
            for mf in range(2):
                ft = fg * 2 + mf
                psp = B.ps[2 + (mf % 2)]
                for kd in range(NT):
                    wts = [ld]
                    if kd == 0:
                        wts.append(("act", p.tok("act")))
                    p.pe(MM(psp[:, :], cb[par][:, kd * 256 + mf * 128:
                                               kd * 256 + mf * 128 + 128],
                            ysl(B, kd, tc * TC, TC), kd == 0, kd == NT - 1),
                         waits=wts, tick=(kd == NT - 1))
                tk_mm = p.tok("pe")
                cb_guard[par] = ("pe", tk_mm)
                h_evac_last = ("act", p.act(
                    ACTV(B.bf(A2 + ft * 1024, 512), psp[:, :], AF.Gelu,
                         bias=b1_sb[:, ft:ft + 1]),
                    waits=[("pe", tk_mm), io["bias_tok"]], tick=True))
        for m in range(NT):
            psp = B.ps[5 + (m % 2)]
            for kh in range(2):
                par = ib % 3
                ib += 1
                ld = p.dma1("pool", DMA(
                    cb[par][:, 0:32 * 128].rearrange("p (k m) -> p k m", m=128),
                    io["w2q"].ap()[kh * 4096:(kh + 1) * 4096,
                                   m * 128:(m + 1) * 128].rearrange(
                        "(k p) m -> p k m", p=128)),
                    waits=[cb_guard[par], tok_w2_done])
                for fk in range(32):
                    fa = kh * 32 + fk
                    wts = [ld, h_evac_last]
                    if fa == 0:
                        wts.append(("dve", p.tok("dve")))
                    p.pe(MM(psp[:, :], cb[par][:, fk * 128:(fk + 1) * 128],
                            B.bf(A2 + fa * 1024, 512), fa == 0, fa == 63),
                         waits=wts, tick=(fa == 63 or fk == 31))
                cb_guard[par] = ("pe", p.tok("pe"))
            tk_mm = p.tok("pe")
            xop = it % 2
            ldx = p.dma1("sp", DMA(xold[xop], xin.ap()[m * 128:(m + 1) * 128,
                                                       tc * TC:(tc + 1) * TC]),
                         waits=[("dve", xold_guard[xop])])
            w2w = [("pe", tk_mm), ldx, io["bias_tok"]]
            if xnew_guard[xop][0] is not None:
                w2w.append(xnew_guard[xop])
            tk_add = p.dve(STT(xnew[xop], psp[:, :], b2_sb[:, m:m + 1],
                               xold[xop], ALU.add, ALU.add),
                           waits=w2w, tick=True)
            xold_guard[xop] = tk_add
            xnew_guard[xop] = p.dma1("sp", DMA(
                xout.ap()[m * 128:(m + 1) * 128, tc * TC:(tc + 1) * TC],
                xnew[xop]), waits=[("dve", tk_add)])
            it += 1
        h_guard = ("pe", p.tok("pe"))


def emit_moe(p, B, io):
    xin, xout = io["xin"], io["xout"]
    wg, wgu, wd = io["wg"], io["wgu"], io["wd"]
    wsg, wsd = io["wsg"], io["wsd"]
    routed_h = io["routed_h"]

    setup_const(p, B)
    tk_y = emit_ln(p, B, xin)

    wg_sb = B.bf(A4, 128)
    tk_wg = p.dma1("pool", DMA(wg_sb[:].rearrange("p (k e) -> p k e", e=8),
                               wg.ap().rearrange("(k p) e -> p k e", p=128)))
    id_sb = B.msk.bitcast(F32)[:, 0:128]
    tk_id = p.dma1("sp", DMA(id_sb, io["id128"].ap()))
    sel_tok = p.dma1("sp", DMA(B.sel[:, :], io["sel8"].ap()))

    lg = B.ev[:, 1024:1032]
    e1 = B.ev[:, 1040:1048]
    mk = B.ev[:, 1056:1064]
    e2 = B.ev[:, 1072:1080]
    c1 = B.ev[:, 1088:1096]
    comb = B.ev[:, 1104:1112]
    m1c, m2c, d12, w1c, w2c = (B.colA[:, 4:5], B.colA[:, 5:6], B.colA[:, 6:7],
                               B.colA[:, 7:8], B.colA[:, 8:9])
    for m in range(NT):
        psp = B.ps[4]
        for kd in range(NT):
            wts = [tk_wg, ("pool", p.k_const)]
            if kd == 0:
                wts.append(("dve", p.tok("dve")))
            p.pe(MM(psp[:, 0:8], ysl(B, kd, m * 128, 128),
                    wg_sb[:, kd * 8:(kd + 1) * 8], kd == 0, kd == NT - 1),
                 waits=wts, tick=(kd == NT - 1))
        tk_mm = p.tok("pe")
        t1_ = p.dve(CP(lg, psp[:, 0:8]), waits=[("pe", tk_mm)], tick=True)
        t2_ = p.dve(RMAX(m1c, lg), waits=[("dve", t1_)], tick=True)
        t3_ = p.dve(TS1(e1, lg, m1c, ALU.is_equal),
                    waits=[("dve", t2_)], tick=True)
        t4_ = p.dve(STT(mk, e1, -1e30, lg, ALU.mult, ALU.add),
                    waits=[("dve", t3_)], tick=True)
        t5_ = p.dve(RMAX(m2c, mk), waits=[("dve", t4_)], tick=True)
        t6_ = p.dve(TS1(e2, mk, m2c, ALU.is_equal), waits=[("dve", t5_)],
                    tick=True)
        t7_ = p.dve(TT(d12, m1c, m2c, ALU.subtract),
                    waits=[("dve", t6_)], tick=True)
        tk_s = p.act(ACTV(w1c, d12, AF.Sigmoid), waits=[("dve", t7_)],
                     tick=True)
        t8_ = p.dve(TS2(w2c, w1c, -1.0, 1.0, ALU.mult, ALU.add),
                    waits=[("act", tk_s)], tick=True)
        t9_ = p.dve(TS1(c1, e1, w1c, ALU.mult), waits=[("dve", t8_)],
                    tick=True)
        tk_cb = p.dve(STT(comb, e2, w2c, c1, ALU.mult, ALU.add),
                      waits=[("dve", t9_)], tick=True)
        tk_tr = p.pe((lambda cm=comb: lambda e: e.transpose(
            B.ps[7][0:8, 0:128], cm, id_sb))(),
            waits=[("dve", tk_cb), tk_id], tick=True)
        p.dve(CP(B.gsc[0:8, m * 128:(m + 1) * 128], B.ps[7][0:8, 0:128]),
              waits=[("pe", tk_tr)], tick=True)
    tk_comb = p.tok("dve")

    gub = [B.bf(A3 + i * 8192, 16 * 256) for i in range(3)]
    wdb = [B.bf(A2 + 32768 + 16384 + i * 2048, 1024) for i in range(2)]
    gbuf = [B.bf(A2 + 32768 + i * 1024, 512) for i in range(4)]
    crep = B.ev[:, 0:512]
    tmpf = B.ev[:, 512:1024]
    gu_guard = ("pe", p.tok("pe"))
    wd_guard = [("pe", 0), ("pe", 0)]
    act_guard = ("pe", p.tok("pe"))
    rout_st = [(None, 0), (None, 0)]
    xold = B.ev[:, 1024:1536]
    xold_guard = p.tok("dve")
    xnew = B.evb.bitcast(F32)[:, 0:512]
    xnew_guard = (None, 0)
    for tc in range(NCH):
        ib = getattr(p, "_moe_ib", 0)
        gu_guard3 = getattr(p, "_moe_gu3", [("pe", 0)] * 3)
        for e in range(8):
            tk_cr = p.pe(MM(B.ps[1][:, :], B.sel[:, e * 128:(e + 1) * 128],
                            B.gsc[0:8, tc * TC:(tc + 1) * TC], True, True),
                         waits=[("dve", tk_comb), sel_tok,
                                ("dve", p.tok("dve"))], tick=True)
            tk_ce = p.dve(CP(crep, B.ps[1][:, :]),
                          waits=[("pe", tk_cr), rout_st[0]], tick=True)
            for fg in range(8):
                par = ib % 3
                ib += 1
                ld = p.dma1("pool", DMA(
                    gub[par][:].rearrange("p (k m) -> p k m", m=256),
                    wgu.ap()[e, :, fg * 256:(fg + 1) * 256].rearrange(
                        "(k p) m -> p k m", p=128)),
                    waits=[gu_guard3[par], ("dve", tk_y)])
                for sub in range(2):
                    fi = fg * 2 + sub
                    psp = B.ps[2 + (sub % 2)]
                    for kd in range(NT):
                        wts = [ld]
                        if kd == 0:
                            wts.append(("act", p.tok("act")))
                            wts.append(("dve", p.tok("dve")))
                        p.pe(MM(psp[:, :], gub[par][:, kd * 256 + sub * 128:
                                                    kd * 256 + sub * 128 + 128],
                                ysl(B, kd, tc * TC, TC), kd == 0, kd == NT - 1),
                             waits=wts, tick=(kd == NT - 1))
                    tk_mm = p.tok("pe")
                    gu_guard3[par] = ("pe", tk_mm)
                    if fi < 8:
                        p.act(ACTV(B.bf(A2 + (e * 8 + fi) * 1024, 512),
                                   psp[:, :], AF.Silu),
                              waits=[("pe", tk_mm)], tick=True)
                    else:
                        g_sl = B.bf(A2 + (e * 8 + fi - 8) * 1024, 512)
                        tk_a1 = p.dve(TT(tmpf, psp[:, :], g_sl, ALU.mult),
                                      waits=[("pe", tk_mm),
                                             ("act", p.tok("act"))], tick=True)
                        p.dve(TT(g_sl, tmpf, crep, ALU.mult),
                              waits=[("dve", tk_a1)], tick=True)
        p._moe_ib, p._moe_gu3 = ib, gu_guard3
        act_done = p.tok("dve")
        if tc == 0 and "dbg_acts" in io:
            st0 = (None, 0)
            for c8 in range(8):
                p.dma("sp", DMA(io["dbg_acts"].ap()[:, c8 * 4096:(c8 + 1) * 4096],
                                B.bf(A2 + c8 * 8192, 4096)),
                      waits=[("dve", act_done)])
            io["dbg_acts_tok"] = p.end_ep("sp")
        wdall = [B.bf(A3 + i * 8192, 4096) for i in range(3)]
        ibd = getattr(p, "_moe_ibd", 0)
        wdall_guard = getattr(p, "_moe_wdg", [gu_guard3[0], gu_guard3[1],
                                              gu_guard3[2]])
        for hm in range(NT):
            psp = B.ps[5 + (hm % 2)]
            first = True
            for eh in range(2):
                par = ibd % 3
                ibd += 1
                ld = p.dma1("pool", DMA(
                    wdall[par][:].rearrange("p (e k m) -> p e k m", e=4, m=128),
                    wd.ap()[eh * 4:(eh + 1) * 4, :,
                            hm * 128:(hm + 1) * 128].rearrange(
                        "e (k p) m -> p e k m", p=128)),
                    waits=[wdall_guard[par]])
                for e4 in range(4):
                    e = eh * 4 + e4
                    for i in range(8):
                        wts = [ld, ("dve", act_done)]
                        if first:
                            wts.append(("dve", p.tok("dve")))
                        p.pe(MM(psp[:, :],
                                wdall[par][:, (e4 * 8 + i) * 128:
                                           (e4 * 8 + i) * 128 + 128],
                                B.bf(A2 + (e * 8 + i) * 1024, 512),
                                first, e == 7 and i == 7),
                             waits=wts, tick=(i == 7))
                        first = False
                wdall_guard[par] = ("pe", p.tok("pe"))
            tk_mm = p.tok("pe")
            rpar = hm % 2
            rslot = B.ev[:, 1536:2048] if rpar else B.ev[:, 0:512]
            tk_ev = p.dve(CP(rslot, psp[:, :]),
                          waits=[("pe", tk_mm), rout_st[rpar]], tick=True)
            rout_st[rpar] = p.dma1("sp", DMA(
                routed_h.ap()[hm * 128:(hm + 1) * 128, tc * TC:(tc + 1) * TC],
                rslot), waits=[("dve", tk_ev)])
        p._moe_ibd, p._moe_wdg = ibd, wdall_guard
        gu_guard3[0] = gu_guard3[1] = gu_guard3[2] = ("pe", p.tok("pe"))
        sh_guard = ("pe", p.tok("pe"))
        for fg in range(16):
            par = ib % 3
            ib += 1
            ldg = p.dma1("pool", DMA(
                gub[par][:].rearrange("p (k m) -> p k m", m=256),
                wsg.ap()[:, fg * 256:(fg + 1) * 256].rearrange(
                    "(k p) m -> p k m", p=128)),
                waits=[gu_guard3[par], sh_guard])
            silu_toks = []
            for sub in range(2):
                psp = B.ps[2 + (sub % 2)]
                for kd in range(NT):
                    wts = [ldg]
                    if kd == 0:
                        wts.append(("act", p.tok("act")))
                        wts.append(("dve", p.tok("dve")))
                    p.pe(MM(psp[:, :], gub[par][:, kd * 256 + sub * 128:
                                                kd * 256 + sub * 128 + 128],
                            ysl(B, kd, tc * TC, TC), kd == 0, kd == NT - 1),
                         waits=wts, tick=(kd == NT - 1))
                tk_mm = p.tok("pe")
                gu_guard3[par] = ("pe", tk_mm)
                silu_toks.append(p.act(ACTV(gbuf[sub], psp[:, :], AF.Silu),
                                       waits=[("pe", tk_mm),
                                              ("dve", p.tok("dve"))],
                                       tick=True))
            par = ib % 3
            ib += 1
            ldu = p.dma1("pool", DMA(
                gub[par][:].rearrange("p (k m) -> p k m", m=256),
                wsg.ap()[:, 4096 + fg * 256:4096 + (fg + 1) * 256].rearrange(
                    "(k p) m -> p k m", p=128)),
                waits=[gu_guard3[par]])
            for sub in range(2):
                it_i = fg * 2 + sub
                psp = B.ps[4 + (sub % 2)]
                for kd in range(NT):
                    wts = [ldu]
                    if kd == 0:
                        wts.append(("act", p.tok("act")))
                        wts.append(("dve", p.tok("dve")))
                    p.pe(MM(psp[:, :], gub[par][:, kd * 256 + sub * 128:
                                                kd * 256 + sub * 128 + 128],
                            ysl(B, kd, tc * TC, TC), kd == 0, kd == NT - 1),
                         waits=wts, tick=(kd == NT - 1))
                tk_mm = p.tok("pe")
                gu_guard3[par] = ("pe", tk_mm)
                p.dve(TT(B.bf(A2 + it_i * 1024, 512), psp[:, :], gbuf[sub],
                         ALU.mult),
                      waits=[("pe", tk_mm), ("act", silu_toks[sub])], tick=True)
        p._moe_ib, p._moe_gu3 = ib, gu_guard3
        actsh_done = p.tok("dve")
        shd = [B.bf(A2 + 36864, 32 * 128), B.bf(A2 + 53248, 32 * 128)]
        shd_guard = [wd_guard[0], wd_guard[1]]
        for hm in range(NT):
            par2 = hm % 2
            ld = p.dma1("pool", DMA(
                shd[par2][:].rearrange("p (k m) -> p k m", m=128),
                wsd.ap()[:, hm * 128:(hm + 1) * 128].rearrange(
                    "(k p) m -> p k m", p=128)),
                waits=[shd_guard[par2], ("pe", p.tok("pe")) if hm == 0
                       else (None, 0), ("dve", actsh_done)])
            psp = B.ps[5 + (hm % 2)]
            for i in range(32):
                wts = [ld, ("dve", actsh_done)]
                if i == 0:
                    wts.append(("dve", p.tok("dve")))
                p.pe(MM(psp[:, :],
                        shd[par2][:, i * 128:(i + 1) * 128],
                        B.bf(A2 + i * 1024, 512), i == 0, i == 31),
                     waits=wts, tick=(i == 31))
            shd_guard[par2] = ("pe", p.tok("pe"))
            tk_mm = p.tok("pe")
            wd_guard = [("pe", tk_mm), ("pe", tk_mm)]
            rpar = hm % 2
            rslot = B.ev[:, 1536:2048] if rpar else B.ev[:, 0:512]
            ldr = p.dma1("sp", DMA(
                rslot, routed_h.ap()[hm * 128:(hm + 1) * 128,
                                     tc * TC:(tc + 1) * TC]),
                waits=[rout_st[rpar], ("dve", p.tok("dve"))])
            rout_st[rpar] = ldr
            ldx = p.dma1("sp", DMA(xold, xin.ap()[hm * 128:(hm + 1) * 128,
                                                  tc * TC:(tc + 1) * TC]),
                         waits=[("dve", xold_guard)])
            tk_t = p.dve(TT(tmpf, psp[:, :], rslot, ALU.add),
                         waits=[("pe", tk_mm), ldr, ("dve", p.tok("dve"))],
                         tick=True)
            w2w = [("dve", tk_t), ldx]
            if xnew_guard[0] is not None:
                w2w.append(xnew_guard)
            tk_add = p.dve(TT(xnew, tmpf, xold, ALU.add), waits=w2w, tick=True)
            xold_guard = tk_add
            xnew_guard = p.dma1("sp", DMA(
                xout.ap()[hm * 128:(hm + 1) * 128, tc * TC:(tc + 1) * TC],
                xnew), waits=[("dve", tk_add)])


def np_bf16(a):
    return np.ascontiguousarray(np.asarray(a).astype(ml_dtypes.bfloat16))


def f32c(a):
    return np.ascontiguousarray(np.asarray(a, dtype=np.float32))


def make_mask4():
    r = np.arange(128)[:, None]
    c = np.arange(512)[None, :]
    m = np.zeros((128, 2048), np.float32)
    for u in range(4):
        m[:, u * 512:(u + 1) * 512] = np.where(r <= c - 128 * u, 0.0, -1e30)
    return np_bf16(m)


def rope_tables(dh, style):
    t = np.arange(T, dtype=np.float32)
    half = dh // 2
    if style == "half":
        inv = 1.0 / (10000.0 ** (np.arange(0, dh, 2, dtype=np.float32) / dh))
    else:
        inv = 1.0 / (10000.0 ** (np.arange(half, dtype=np.float32) / half))
    fr = t[:, None] * inv[None, :]
    C = np.concatenate([np.cos(fr).T, np.cos(fr).T], 0)
    S = np.concatenate([-np.sin(fr).T, np.sin(fr).T], 0)
    return f32c(C), f32c(S)


def deinterleave_cols(W, H, dh):
    perm = np.concatenate([np.arange(0, dh, 2), np.arange(1, dh, 2)])
    W4 = W.reshape(W.shape[0], H, dh)[:, :, perm]
    return f32c(W4.reshape(W.shape[0], H * dh))


_PROG_CACHE = {}


def build_attn_program(key, cfg):
    if key in _PROG_CACHE:
        return _PROG_CACHE[key]
    nc = bass.Bass("TRN2", target_bir_lowering=False)
    H, dh, KVH = cfg["H"], cfg["dh"], cfg["KVH"]
    WV = KVH * dh
    io = {}
    io["xin"] = nc.dram_tensor("xin", [D, T], F32, kind="ExternalInput")
    io["xout"] = nc.dram_tensor("xout", [D, T], F32, kind="ExternalOutput")
    io["wq"] = nc.dram_tensor("wq", [D, H * dh], F32, kind="ExternalInput")
    io["wk"] = nc.dram_tensor("wk", [D, WV], F32, kind="ExternalInput")
    io["wv"] = nc.dram_tensor("wv", [D, WV], F32, kind="ExternalInput")
    io["wo"] = nc.dram_tensor("wo", [H * dh, D], F32, kind="ExternalInput")
    io["ctxh"] = nc.dram_tensor("ctxh", [H * dh, T], BF16, kind="Internal")
    if cfg["causal"]:
        io["mask4"] = nc.dram_tensor("mask4", [128, 2048], BF16,
                                     kind="ExternalInput")
    if cfg.get("rope"):
        io["ropeC"] = nc.dram_tensor("ropeC", [dh, T], F32, kind="ExternalInput")
        io["ropeS"] = nc.dram_tensor("ropeS", [dh, T], F32, kind="ExternalInput")
    if cfg.get("gsa"):
        io["spos"] = nc.dram_tensor("spos", [16, 8, 128], F32, kind="ExternalInput")
        io["sdir"] = nc.dram_tensor("sdir", [16, 8, 128], F32, kind="ExternalInput")
        io["sposT"] = nc.dram_tensor("sposT", [16, 128, 8], F32, kind="ExternalInput")
        io["sdirT"] = nc.dram_tensor("sdirT", [16, 128, 8], F32, kind="ExternalInput")
        io["lsc"] = nc.dram_tensor("lsc", [16, 8], F32, kind="ExternalInput")
        io["lam"] = nc.dram_tensor("lam", [16, 8], F32, kind="ExternalInput")
        io["dstr"] = nc.dram_tensor("dstr", [1, 1], F32, kind="ExternalInput")
        io["gstr"] = nc.dram_tensor("gstr", [1, 1], F32, kind="ExternalInput")
        io["id8"] = nc.dram_tensor("id8", [8, 8], F32, kind="ExternalInput")
    with contextlib.ExitStack() as stack:
        p = Plan()
        B = Bufs(nc, stack)
        if cfg.get("bias"):
            bq_d = nc.dram_tensor("bq", [dh, H], F32, kind="ExternalInput")
            bk_d = nc.dram_tensor("bk", [dh, KVH], F32, kind="ExternalInput")
            bv_d = nc.dram_tensor("bv", [1, WV], F32, kind="ExternalInput")
            bo_d = nc.dram_tensor("bo", [128, 16], F32, kind="ExternalInput")
            bq_sb = stack.enter_context(nc.sbuf_tensor("bq_sb", [dh, H], F32))
            bk_sb = stack.enter_context(nc.sbuf_tensor("bk_sb", [dh, KVH], F32))
            bo_sb = stack.enter_context(nc.sbuf_tensor("bo_sb", [128, 16], F32))
            p.dma("sp", DMA(bq_sb[:, :], bq_d.ap()))
            p.dma("sp", DMA(bk_sb[:, :], bk_d.ap()))
            p.dma("sp", DMA(bo_sb[:, :], bo_d.ap()))
            io["bias_tok"] = p.end_ep("sp")
            io.update(bv=bv_d, bq_sb=bq_sb, bk_sb=bk_sb, bo_sb=bo_sb)
        emit_attn(p, B, io, cfg)
        p.emit(nc)
    _PROG_CACHE[key] = nc
    return nc


def attn_host_inputs(cfg, x_fm, Wq, Wk, Wv, Wo, bq=None, bk=None, bv=None,
                     bo=None, pre_arranged=False):
    H, dh, KVH = cfg["H"], cfg["dh"], cfg["KVH"]
    m = {"xin": x_fm, "wq": f32c(Wq), "wk": f32c(Wk), "wv": f32c(Wv),
         "wo": f32c(Wo)}
    if cfg["causal"]:
        m["mask4"] = make_mask4()
    if cfg.get("rope"):
        C, S = rope_tables(dh, cfg["rope"])
        m["ropeC"], m["ropeS"] = C, S
    if cfg.get("bias"):
        m["bq"] = f32c(np.asarray(bq).reshape(H, dh).T)
        m["bk"] = f32c(np.asarray(bk).reshape(KVH, dh).T)
        m["bv"] = f32c(np.asarray(bv).reshape(1, KVH * dh))
        m["bo"] = f32c(np.asarray(bo).reshape(16, 128).T)
    return m


def gsa_host_inputs(spos, sdir, lsc, lam, dstr, gstr):
    return {"spos": f32c(spos), "sdir": f32c(sdir),
            "sposT": f32c(np.asarray(spos).transpose(0, 2, 1)),
            "sdirT": f32c(np.asarray(sdir).transpose(0, 2, 1)),
            "lsc": f32c(lsc), "lam": f32c(lam),
            "dstr": f32c(np.asarray(dstr).reshape(1, 1)),
            "gstr": f32c(np.asarray(gstr).reshape(1, 1)),
            "id8": np.eye(8, dtype=np.float32)}




def build_ffn_program():
    if "ffn0" in _PROG_CACHE:
        return _PROG_CACHE["ffn0"]
    nc = bass.Bass("TRN2", target_bir_lowering=False)
    io = {}
    io["xin"] = nc.dram_tensor("xin", [D, T], F32, kind="ExternalInput")
    io["xout"] = nc.dram_tensor("xout", [D, T], F32, kind="ExternalOutput")
    io["w1t"] = nc.dram_tensor("w1t", [2048, 8192], F32, kind="ExternalInput")
    io["w2t"] = nc.dram_tensor("w2t", [8192, 2048], F32, kind="ExternalInput")
    io["w1q"] = nc.dram_tensor("w1q", [2048, 8192], BF16, kind="Internal")
    io["w2q"] = nc.dram_tensor("w2q", [8192, 2048], BF16, kind="Internal")
    io["acc_hbm"] = nc.dram_tensor("acc_hbm", [1, 128], F32, kind="Internal")
    b1_d = nc.dram_tensor("b1", [128, 64], F32, kind="ExternalInput")
    b2_d = nc.dram_tensor("b2", [128, 16], F32, kind="ExternalInput")
    with contextlib.ExitStack() as stack:
        p = Plan()
        B = Bufs(nc, stack)
        b1_sb = stack.enter_context(nc.sbuf_tensor("b1_sb", [128, 64], F32))
        b2_sb = stack.enter_context(nc.sbuf_tensor("b2_sb", [128, 16], F32))
        p.dma("sp", DMA(b1_sb[:, :], b1_d.ap()))
        p.dma("sp", DMA(b2_sb[:, :], b2_d.ap()))
        io["bias_tok"] = p.end_ep("sp")
        io["b1_sb"], io["b2_sb"] = b1_sb, b2_sb
        emit_ffn0(p, B, io)
        p.emit(nc)
    _PROG_CACHE["ffn0"] = nc
    return nc


def build_moe_program():
    if "moe" in _PROG_CACHE:
        return _PROG_CACHE["moe"]
    nc = bass.Bass("TRN2", target_bir_lowering=False)
    io = {}
    io["xin"] = nc.dram_tensor("xin", [D, T], F32, kind="ExternalInput")
    io["xout"] = nc.dram_tensor("xout", [D, T], F32, kind="ExternalOutput")
    io["wg"] = nc.dram_tensor("wg", [2048, 8], F32, kind="ExternalInput")
    io["wgu"] = nc.dram_tensor("wgu", [8, 2048, 2048], F32, kind="ExternalInput")
    io["wd"] = nc.dram_tensor("wd", [8, 1024, 2048], F32, kind="ExternalInput")
    io["wsg"] = nc.dram_tensor("wsg", [2048, 8192], F32, kind="ExternalInput")
    io["wsd"] = nc.dram_tensor("wsd", [4096, 2048], F32, kind="ExternalInput")
    io["id128"] = nc.dram_tensor("id128", [128, 128], F32, kind="ExternalInput")
    io["sel8"] = nc.dram_tensor("sel8", [8, 1024], F32, kind="ExternalInput")
    io["routed_h"] = nc.dram_tensor("routed_h", [D, T], F32, kind="Internal")
    with contextlib.ExitStack() as stack:
        p = Plan()
        B = Bufs(nc, stack)
        emit_moe(p, B, io)
        p.emit(nc)
    _PROG_CACHE["moe"] = nc
    return nc


def moe_host_inputs(x_fm, wg, wgu, wd, wsg, wsd):
    sel = np.zeros((8, 1024), np.float32)
    for e in range(8):
        sel[e, e * 128:(e + 1) * 128] = 1.0
    return {"xin": x_fm, "wg": f32c(wg), "wgu": f32c(wgu), "wd": f32c(wd),
            "wsg": f32c(wsg), "wsd": f32c(wsd),
            "id128": np.eye(128, dtype=np.float32), "sel8": sel}


def ffn_host_inputs(x_fm, w1, b1, w2, b2):
    return {"xin": x_fm, "w1t": f32c(np.asarray(w1).T),
            "w2t": f32c(np.asarray(w2).T),
            "b1": f32c(np.asarray(b1).reshape(64, 128).T),
            "b2": f32c(np.asarray(b2).reshape(16, 128).T)}


def run_spmd(nc, in_maps):
    res = run_bass_kernel_spmd(nc, in_maps, core_ids=list(range(8)))
    return res.results


def build_fused_program():
    if "fused" in _PROG_CACHE:
        return _PROG_CACHE["fused"]
    nc = bass.Bass("TRN2", target_bir_lowering=False)
    xb = [nc.dram_tensor("xin", [D, T], F32, kind="ExternalInput")]
    for i in range(1, 8):
        xb.append(nc.dram_tensor("x%d" % i, [D, T], F32, kind="Internal"))
    xb.append(nc.dram_tensor("xout", [D, T], F32, kind="ExternalOutput"))
    ctxh = nc.dram_tensor("ctxh", [2048, T], BF16, kind="Internal")

    def dt(name, shape):
        return nc.dram_tensor(name, shape, F32, kind="ExternalInput")

    io_a = {"wq": dt("a_wq", [D, D]), "wk": dt("a_wk", [D, D]),
            "wv": dt("a_wv", [D, D]), "wo": dt("a_wo", [D, D]),
            "spos": dt("spos", [16, 8, 128]), "sdir": dt("sdir", [16, 8, 128]),
            "sposT": dt("sposT", [16, 128, 8]), "sdirT": dt("sdirT", [16, 128, 8]),
            "lsc": dt("lsc", [16, 8]), "lam": dt("lam", [16, 8]),
            "dstr": dt("dstr", [1, 1]), "gstr": dt("gstr", [1, 1]),
            "id8": dt("id8", [8, 8]), "ctxh": ctxh}
    io_f = {"w1t": dt("f_w1t", [2048, 8192]), "w2t": dt("f_w2t", [8192, 2048]),
            "w1q": nc.dram_tensor("f_w1q", [2048, 8192], BF16, kind="Internal"),
            "w2q": nc.dram_tensor("f_w2q", [8192, 2048], BF16, kind="Internal"),
            "acc_hbm": nc.dram_tensor("f_acc", [1, 128], F32, kind="Internal")}
    io_b = {"wq": dt("b_wq", [D, D]), "wk": dt("b_wk", [D, D]),
            "wv": dt("b_wv", [D, D]), "wo": dt("b_wo", [D, D]),
            "mask4": nc.dram_tensor("mask4", [128, 2048], BF16,
                                    kind="ExternalInput"),
            "ropeC": dt("b_ropeC", [128, T]), "ropeS": dt("b_ropeS", [128, T]),
            "ctxh": ctxh}
    io_c = {"wq": dt("c_wq", [D, D]), "wk": dt("c_wk", [D, D]),
            "wv": dt("c_wv", [D, D]), "wo": dt("c_wo", [D, D]),
            "mask4": io_b["mask4"], "ctxh": ctxh}
    io_m = {"wg": dt("m_wg", [2048, 8]), "wgu": dt("m_wgu", [8, 2048, 2048]),
            "wd": dt("m_wd", [8, 1024, 2048]), "wsg": dt("m_wsg", [2048, 8192]),
            "wsd": dt("m_wsd", [4096, 2048]), "id128": dt("id128", [128, 128]),
            "sel8": dt("sel8", [8, 1024]),
            "routed_h": nc.dram_tensor("m_routed", [D, T], F32, kind="Internal")}
    io_d = {"wq": dt("d_wq", [D, D]), "wk": dt("d_wk", [D, 512]),
            "wv": dt("d_wv", [D, 512]), "wo": dt("d_wo", [D, D]),
            "mask4": io_b["mask4"],
            "ropeC": dt("d_ropeC", [64, T]), "ropeS": dt("d_ropeS", [64, T]),
            "ctxh": ctxh}
    with contextlib.ExitStack() as stack:
        p = Plan()
        B = Bufs(nc, stack)
        bq_b = dt("b_bq", [128, 16]); bk_b = dt("b_bk", [128, 16])
        bv_b = dt("b_bv", [1, 2048]); bo_b = dt("b_bo", [128, 16])
        bq_c = dt("c_bq", [128, 16]); bk_c = dt("c_bk", [128, 16])
        bv_c = dt("c_bv", [1, 2048]); bo_c = dt("c_bo", [128, 16])
        b1_d = dt("f_b1", [128, 64]); b2_d = dt("f_b2", [128, 16])
        bq_sb = stack.enter_context(nc.sbuf_tensor("bq_sb", [128, 16], F32))
        bk_sb = stack.enter_context(nc.sbuf_tensor("bk_sb", [128, 16], F32))
        bo_sb = stack.enter_context(nc.sbuf_tensor("bo_sb", [128, 16], F32))
        b1_sb = stack.enter_context(nc.sbuf_tensor("b1_sb", [128, 64], F32))
        b2_sb = stack.enter_context(nc.sbuf_tensor("b2_sb", [128, 16], F32))
        io_f["b1_sb"], io_f["b2_sb"] = b1_sb, b2_sb

        def load_bias(io, bq_d, bk_d, bv_d, bo_d):
            p.dma("sp", DMA(bq_sb[:, :], bq_d.ap()))
            p.dma("sp", DMA(bk_sb[:, :], bk_d.ap()))
            p.dma("sp", DMA(bo_sb[:, :], bo_d.ap()))
            io["bias_tok"] = p.end_ep("sp")
            io.update(bv=bv_d, bq_sb=bq_sb, bk_sb=bk_sb, bo_sb=bo_sb)

        io_a.update(xin=xb[0], xout=xb[1])
        emit_attn(p, B, io_a, CFG_A)
        p.set_barrier()
        io_f.update(xin=xb[1], xout=xb[2])
        p.dma("sp", DMA(b1_sb[:, :], b1_d.ap()))
        p.dma("sp", DMA(b2_sb[:, :], b2_d.ap()))
        io_f["bias_tok"] = p.end_ep("sp")
        emit_ffn0(p, B, io_f, do_quant=True)
        p.set_barrier()
        io_b.update(xin=xb[2], xout=xb[3])
        load_bias(io_b, bq_b, bk_b, bv_b, bo_b)
        emit_attn(p, B, io_b, CFG_B)
        p.set_barrier()
        io_f2 = dict(io_f, xin=xb[3], xout=xb[4])
        emit_ffn0(p, B, io_f2, do_quant=False)
        p.set_barrier()
        io_c.update(xin=xb[4], xout=xb[5])
        load_bias(io_c, bq_c, bk_c, bv_c, bo_c)
        emit_attn(p, B, io_c, CFG_C)
        p.set_barrier()
        io_m.update(xin=xb[5], xout=xb[6])
        emit_moe(p, B, io_m)
        p.set_barrier()
        io_d.update(xin=xb[6], xout=xb[7])
        emit_attn(p, B, io_d, CFG_D)
        p.set_barrier()
        io_m2 = dict(io_m, xin=xb[7], xout=xb[8])
        emit_moe(p, B, io_m2)
        p.emit(nc)
    _PROG_CACHE["fused"] = nc
    return nc


def fused_host_inputs(inputs, x_fm):
    f = lambda k: f32c(inputs[k])
    m = {"xin": x_fm}
    m.update({"a_wq": f("L0_attn_a_Wq"), "a_wk": f("L0_attn_a_Wk"),
              "a_wv": f("L0_attn_a_Wv"), "a_wo": f("L0_attn_a_Wo")})
    g = gsa_host_inputs(f("L0_attn_a_splat_positions"),
                        f("L0_attn_a_splat_directions"),
                        f("L0_attn_a_splat_log_scales"),
                        f("L0_attn_a_splat_log_amplitudes"),
                        f("L0_attn_a_directional_strength"),
                        f("L0_attn_a_gsa_strength"))
    m.update(g)
    m.update({"f_w1t": f32c(f("L0_ffn_w1").T), "f_w2t": f32c(f("L0_ffn_w2").T),
              "f_b1": f32c(np.asarray(inputs["L0_ffn_b1"]).reshape(64, 128).T),
              "f_b2": f32c(np.asarray(inputs["L0_ffn_b2"]).reshape(16, 128).T)})
    C, S = rope_tables(128, "half")
    m.update({"b_wq": f32c(f("L0_attn_b_Wq").T), "b_wk": f32c(f("L0_attn_b_Wk").T),
              "b_wv": f32c(f("L0_attn_b_Wv").T), "b_wo": f32c(f("L0_attn_b_Wo").T),
              "b_ropeC": C, "b_ropeS": S, "mask4": make_mask4(),
              "b_bq": f32c(np.asarray(inputs["L0_attn_b_bq"]).reshape(16, 128).T),
              "b_bk": f32c(np.asarray(inputs["L0_attn_b_bk"]).reshape(16, 128).T),
              "b_bv": f32c(np.asarray(inputs["L0_attn_b_bv"]).reshape(1, 2048)),
              "b_bo": f32c(np.asarray(inputs["L0_attn_b_bo"]).reshape(16, 128).T)})
    m.update({"c_wq": f32c(f("L1_attn_a_Wq").T), "c_wk": f32c(f("L1_attn_a_Wk").T),
              "c_wv": f32c(f("L1_attn_a_Wv").T), "c_wo": f32c(f("L1_attn_a_Wo").T),
              "c_bq": f32c(np.asarray(inputs["L1_attn_a_bq"]).reshape(16, 128).T),
              "c_bk": f32c(np.asarray(inputs["L1_attn_a_bk"]).reshape(16, 128).T),
              "c_bv": f32c(np.asarray(inputs["L1_attn_a_bv"]).reshape(1, 2048)),
              "c_bo": f32c(np.asarray(inputs["L1_attn_a_bo"]).reshape(16, 128).T)})
    mm = moe_host_inputs(None, f("L1_ffn_w_gate"), f("L1_ffn_w_gate_up"),
                         f("L1_ffn_w_down"), f("L1_ffn_w_sh_gate_up"),
                         f("L1_ffn_w_sh_down"))
    del mm["xin"]
    m.update({"m_wg": mm["wg"], "m_wgu": mm["wgu"], "m_wd": mm["wd"],
              "m_wsg": mm["wsg"], "m_wsd": mm["wsd"], "id128": mm["id128"],
              "sel8": mm["sel8"]})
    C2, S2 = rope_tables(64, "ileave")
    m.update({"d_wq": deinterleave_cols(f("L1_attn_b_Wq"), 32, 64),
              "d_wk": deinterleave_cols(f("L1_attn_b_Wk"), 8, 64),
              "d_wv": f("L1_attn_b_Wv"), "d_wo": f("L1_attn_b_Wo"),
              "d_ropeC": C2, "d_ropeS": S2})
    return m


def kernel_fused(**inputs):
    x0 = np.asarray(inputs["L0_attn_a_hidden_states"], dtype=np.float32)
    nc = build_fused_program()
    in_maps = [fused_host_inputs(inputs, np.ascontiguousarray(x0[c].T))
               for c in range(8)]
    res = run_spmd(nc, in_maps)
    return np.ascontiguousarray(
        np.stack([np.asarray(res[c]["xout"]).T for c in range(8)],
                 0)).astype(np.float32)


CFG_A = dict(H=16, dh=128, KVH=16, causal=False, rope=None, bias=False, gsa=True)
CFG_B = dict(H=16, dh=128, KVH=16, causal=True, rope="half", bias=True)
CFG_C = dict(H=16, dh=128, KVH=16, causal=True, rope=None, bias=True)
CFG_D = dict(H=32, dh=64, KVH=8, causal=True, rope="ileave", bias=False)


def kernel(**inputs):
    return kernel_fused(**inputs)


def kernel_multilaunch(**inputs):
    f = lambda k: np.asarray(inputs[k], dtype=np.float32)
    x0 = f("L0_attn_a_hidden_states")
    xs = [np.ascontiguousarray(x0[c].T) for c in range(8)]

    nc_a = build_attn_program("attn_a", CFG_A)
    nc_f = build_ffn_program()
    nc_b = build_attn_program("attn_b", CFG_B)
    nc_c = build_attn_program("attn_c", CFG_C)
    nc_m = build_moe_program()
    nc_d = build_attn_program("attn_d", CFG_D)

    def step(nc, maps):
        res = run_spmd(nc, maps)
        return [np.asarray(res[c]["xout"]) for c in range(8)]

    gsa_m = gsa_host_inputs(f("L0_attn_a_splat_positions"),
                            f("L0_attn_a_splat_directions"),
                            f("L0_attn_a_splat_log_scales"),
                            f("L0_attn_a_splat_log_amplitudes"),
                            f("L0_attn_a_directional_strength"),
                            f("L0_attn_a_gsa_strength"))
    maps = []
    for c in range(8):
        m = attn_host_inputs(CFG_A, xs[c], f("L0_attn_a_Wq"), f("L0_attn_a_Wk"),
                             f("L0_attn_a_Wv"), f("L0_attn_a_Wo"))
        m.update(gsa_m)
        maps.append(m)
    xs = step(nc_a, maps)

    ffn_m = ffn_host_inputs(None, f("L0_ffn_w1"), f("L0_ffn_b1"),
                            f("L0_ffn_w2"), f("L0_ffn_b2"))
    del ffn_m["xin"]
    xs = step(nc_f, [dict(ffn_m, xin=xs[c]) for c in range(8)])

    maps = [attn_host_inputs(CFG_B, xs[c], f("L0_attn_b_Wq").T,
                             f("L0_attn_b_Wk").T, f("L0_attn_b_Wv").T,
                             f("L0_attn_b_Wo").T, bq=f("L0_attn_b_bq"),
                             bk=f("L0_attn_b_bk"), bv=f("L0_attn_b_bv"),
                             bo=f("L0_attn_b_bo")) for c in range(8)]
    xs = step(nc_b, maps)

    xs = step(nc_f, [dict(ffn_m, xin=xs[c]) for c in range(8)])

    maps = [attn_host_inputs(CFG_C, xs[c], f("L1_attn_a_Wq").T,
                             f("L1_attn_a_Wk").T, f("L1_attn_a_Wv").T,
                             f("L1_attn_a_Wo").T, bq=f("L1_attn_a_bq"),
                             bk=f("L1_attn_a_bk"), bv=f("L1_attn_a_bv"),
                             bo=f("L1_attn_a_bo")) for c in range(8)]
    xs = step(nc_c, maps)

    moe_m = moe_host_inputs(None, f("L1_ffn_w_gate"), f("L1_ffn_w_gate_up"),
                            f("L1_ffn_w_down"), f("L1_ffn_w_sh_gate_up"),
                            f("L1_ffn_w_sh_down"))
    del moe_m["xin"]
    xs = step(nc_m, [dict(moe_m, xin=xs[c]) for c in range(8)])

    wq_p = deinterleave_cols(f("L1_attn_b_Wq"), 32, 64)
    wk_p = deinterleave_cols(f("L1_attn_b_Wk"), 8, 64)
    maps = [attn_host_inputs(CFG_D, xs[c], wq_p, wk_p, f("L1_attn_b_Wv"),
                             f("L1_attn_b_Wo")) for c in range(8)]
    xs = step(nc_d, maps)

    xs = step(nc_m, [dict(moe_m, xin=xs[c]) for c in range(8)])

    return np.ascontiguousarray(
        np.stack([xs[c].T for c in range(8)], 0)).astype(np.float32)
